# version 11
# baseline (speedup 1.0000x reference)
import math
import numpy as np
import ml_dtypes
import concourse.bass as bass
import concourse.mybir as mybir
from concourse.bass_utils import run_bass_kernel_spmd

F32 = mybir.dt.float32
BF16 = mybir.dt.bfloat16
AF = mybir.ActivationFunctionType
ALU = mybir.AluOpType
AX = mybir.AxisListType

L, D, N, H = 8192, 1024, 64, 4096
T = 512
NT = L // T
NB = T // 128
KD = D // 128
KH = H // 128
RS_A = 4
RS_B = 4
EPS = 1e-5

DEBUG = False


class Plan:

    def __init__(self, nc):
        self.nc = nc
        self.prog = {k: [] for k in ("sync", "gpsimd", "tensor", "vector", "scalar")}
        self.cnt = {}
        self.waited = {}
        self.sems = {}

    def sem(self, name):
        if name not in self.sems:
            self.sems[name] = self.nc.alloc_semaphore(name)
            self.cnt[name] = 0
        return self.sems[name]

    def bump(self, name, k=1):
        self.cnt[name] += k
        return self.cnt[name]

    def op(self, eng, emit, waits=(), inc=None, inc_n=1):
        sems = self.sems
        real_waits = []
        for sname, val in waits:
            if val is None or val <= 0:
                continue
            key = (eng, sname)
            if self.waited.get(key, 0) >= val:
                continue
            self.waited[key] = val
            real_waits.append((sname, val))
        rec = None
        if inc is not None:
            rec = self.bump(inc, inc_n)

        def f(e, real_waits=real_waits, emit=emit, incname=inc, inc_amt=inc_n):
            for sname, val in real_waits:
                e.wait_ge(sems[sname], val)
            ins = emit(e)
            if incname is not None:
                ins.then_inc(sems[incname], inc_amt)

        self.prog[eng].append(f)
        return rec


def build_program(debug=False):
    nc = bass.Bass("TRN2", target_bir_lowering=False)

    u_d = nc.dram_tensor("u", [L, D], F32, kind="ExternalInput")
    wA_h = nc.dram_tensor("wA", [32, 128, 16, 128], F32, kind="ExternalInput")
    wH1_h = nc.dram_tensor("wH1", [32, 128, 32, 128], F32, kind="ExternalInput")
    wH2_h = nc.dram_tensor("wH2", [32, 128, 32, 128], F32, kind="ExternalInput")
    wW3_h = nc.dram_tensor("wW3", [8, 128, 32, 128], F32, kind="ExternalInput")
    BT_d = nc.dram_tensor("BT", [D, N], F32, kind="ExternalInput")
    CT_d = nc.dram_tensor("CT", [N, D], F32, kind="ExternalInput")
    w1s_d = nc.dram_tensor("w1s", [128, 32], F32, kind="ExternalInput")
    w2s_d = nc.dram_tensor("w2s", [128, 32], F32, kind="ExternalInput")
    h1s_d = nc.dram_tensor("h1s", [128, 32], F32, kind="ExternalInput")
    h2s_d = nc.dram_tensor("h2s", [128, 32], F32, kind="ExternalInput")
    w3s_d = nc.dram_tensor("w3s", [128, 8], F32, kind="ExternalInput")
    Dv_d = nc.dram_tensor("Dv", [128, 8], F32, kind="ExternalInput")
    lam_d = nc.dram_tensor("Lam", [N, 1], F32, kind="ExternalInput")
    ld_d = nc.dram_tensor("logdt", [1, 1], F32, kind="ExternalInput")
    idbf_d = nc.dram_tensor("idbf", [128, 128], BF16, kind="ExternalInput")
    idf_d = nc.dram_tensor("idf", [128, 128], F32, kind="ExternalInput")
    out_d = nc.dram_tensor("out", [L, D], F32, kind="ExternalOutput")

    scrA = nc.dram_tensor("scrA", [32, 128, 16, 128], BF16, kind="Internal")
    scrH1 = nc.dram_tensor("scrH1", [32, 128, 32, 128], BF16, kind="Internal")
    scrH2 = nc.dram_tensor("scrH2", [32, 128, 32, 128], BF16, kind="Internal")
    scrW3 = nc.dram_tensor("scrW3", [8, 128, 32, 128], BF16, kind="Internal")

    dbg = {}
    if debug:
        dbg["xnT0"] = nc.dram_tensor("dbg_xnT0", [128, 4096], BF16, kind="ExternalOutput")
        dbg["xs0"] = nc.dram_tensor("dbg_xs0", [N, T], F32, kind="ExternalOutput")
        dbg["s0"] = nc.dram_tensor("dbg_s0", [128, 4096], F32, kind="ExternalOutput")
        dbg["hA0"] = nc.dram_tensor("dbg_hA0", [128, 16384], BF16, kind="ExternalOutput")
        dbg["hB0"] = nc.dram_tensor("dbg_hB0", [128, 16384], BF16, kind="ExternalOutput")
        dbg["sg0"] = nc.dram_tensor("dbg_sg0", [128, 4096], F32, kind="ExternalOutput")
        dbg["stats0"] = nc.dram_tensor("dbg_stats0", [128, 16], F32, kind="ExternalOutput")
        dbg["u0"] = nc.dram_tensor("dbg_u0", [128, 1024], F32, kind="ExternalOutput")

    A = nc.alloc_sbuf_tensor
    u_tok = A("u_tok", [128, NB * D], F32)
    xnT = [A(f"xnT{p}", [128, KD * T], BF16) for p in range(2)]
    xn_blk = [A(f"xn_blk{p}", [128, D], BF16) for p in range(3)]
    h_A = A("h_A", [128, KH * T], BF16)
    h_B = A("h_B", [128, KH * T], BF16)
    s_tile = A("s_tile", [128, KD * T], F32)
    out_tok = A("out_tok", [128, 3 * D], F32)
    ringA = [A(f"ringA{r}", [128, 16 * 128], BF16) for r in range(RS_A)]
    ringB = [A(f"ringB{r}", [128, 32 * 128], BF16) for r in range(RS_B)]
    act_scr = [A(f"act_scr{p}", [128, T], F32) for p in range(2)]
    dve_scr = [A(f"dve_scr{p}", [128, T], F32) for p in range(2)]
    g_scr = A("g_scr", [128, T], F32)
    A_rep = A("A_rep", [N, T], F32)
    xs = A("xs", [N, T], F32)
    xs_bf = A("xs_bf", [N, T], BF16)
    carry = A("carry", [N, 1], F32)
    a_sb = A("a_sb", [N, 1], F32)
    lam_sb = A("lam_sb", [N, 1], F32)
    ld_sb = A("ld_sb", [1, 1], F32)
    lamdt = A("lamdt", [N, 1], F32)
    e1 = A("e1", [N, 1], F32)
    ones1 = A("ones1", [1, N], F32)
    BT_sb = A("BT_sb", [128, KD * N], BF16)
    CT_sb = A("CT_sb", [N, D], BF16)
    w1s = A("w1s_sb", [128, 32], F32)
    w2s = A("w2s_sb", [128, 32], F32)
    h1s = A("h1s_sb", [128, 32], F32)
    h2s = A("h2s_sb", [128, 32], F32)
    w3s = A("w3s_sb", [128, 8], F32)
    Dv = A("Dv_sb", [128, 8], F32)
    s12 = A("s12_sb", [128, 32], F32)
    h1sd = A("h1sd_sb", [128, 32], F32)
    h2sd = A("h2sd_sb", [128, 32], F32)
    idbf = A("idbf_sb", [128, 128], BF16)
    idf = A("idf_sb", [128, 128], F32)
    sums = A("sums", [128, NB], F32)
    sumsq = A("sumsq", [128, NB], F32)
    mcol = A("mcol", [128, NB], F32)
    vpe = A("vpe", [128, NB], F32)
    rstd = A("rstd", [128, NB], F32)
    nmr = A("nmr", [128, NB], F32)
    nt0 = A("nt0", [128, NB], F32)
    nt1 = A("nt1", [128, NB], F32)

    ps_st = [nc.alloc_psum_tensor(f"ps_st{k}", [128, T], F32) for k in range(4)]
    ps_bu = nc.alloc_psum_tensor("ps_bu", [N, T], F32)
    ps_xnT = nc.alloc_psum_tensor("ps_xnT", [128, 1024], BF16)
    ps_cs = [nc.alloc_psum_tensor(f"ps_cs{k}", [128, T], F32) for k in range(2)]

    sbuf_left = nc.sbuf_bytes_remaining

    P = Plan(nc)
    for s in (["sm_castW3", "sm_const", "se_pe", "se_act", "se_dve", "sm_dbg"]
              + [f"sm_cA{c}" for c in range(4)]
              + [f"sm_cH1{c}" for c in range(4)]
              + [f"sm_cH2{c}" for c in range(4)]
              + [f"sm_u_{b}" for b in range(NB)]
              + [f"sm_uo_{b}" for b in range(NB)]
              + [f"sm_sa{r}" for r in range(RS_A)]
              + [f"sm_sb{r}" for r in range(RS_B)]
              + [f"sm_st{b}" for b in range(NB)]):
        P.sem(s)

    rec = {}

    def plan_slabA_load(g):
        if g >= NT * 32:
            return
        i, j = g // 32, g % 32
        waits = []
        if i == 0:
            waits.append((f"sm_cA{j // 8}", 16))
        if g >= RS_A:
            gp = g - RS_A
            waits.append(("se_pe", rec[("Amm", gp // 32, gp % 32)]))
        P.op("sync",
             lambda e, j=j, g=g: e.dma_start(
                 out=ringA[g % RS_A][:],
                 in_=scrA[j].rearrange("p s c -> p (s c)")),
             waits=waits, inc=f"sm_sa{g % RS_A}", inc_n=16)

    def big_rec(g):
        i, pos = g // 72, g % 72
        if pos < 32:
            return rec[("h1mm", i, pos)]
        if pos < 64:
            return rec[("h2mm", i, pos - 32)]
        return rec[("w3mm", i, pos - 64)]

    def plan_slabB_load(g):
        if g >= NT * 72:
            return
        i, pos = g // 72, g % 72
        if pos < 32:
            scr, j, cast, castv = scrH1, pos, f"sm_cH1{pos // 8}", 8 * 16
        elif pos < 64:
            scr, j, cast, castv = scrH2, pos - 32, f"sm_cH2{(pos - 32) // 8}", 8 * 16
        else:
            scr, j, cast, castv = scrW3, pos - 64, "sm_castW3", 8 * 16
        waits = []
        if i == 0:
            waits.append((cast, 16))
        if g >= RS_B:
            waits.append(("se_pe", big_rec(g - RS_B)))
        P.op("sync",
             lambda e, scr=scr, j=j, g=g: e.dma_start(
                 out=ringB[g % RS_B][:],
                 in_=scr[j].rearrange("p kc c -> p (kc c)")),
             waits=waits, inc=f"sm_sb{g % RS_B}", inc_n=16)

    for b in range(NB):
        P.op("sync",
             lambda e, b=b: e.dma_start(
                 out=u_tok[:, b * D:(b + 1) * D],
                 in_=u_d[b * 128:(b + 1) * 128, :]),
             inc=f"sm_u_{b}", inc_n=16)

    def cast_chunk(dst, src, c, nj, sem):
        P.op("gpsimd",
             lambda e, dst=dst, src=src, c=c, nj=nj: e.dma_start(
                 out=dst[c * nj:(c + 1) * nj].rearrange("j p s c -> (j p) (s c)"),
                 in_=src[c * nj:(c + 1) * nj].rearrange("j p s c -> (j p) (s c)")),
             inc=sem, inc_n=16)

    for c in range(4):
        cast_chunk(scrA, wA_h, c, 8, f"sm_cA{c}")
    for c in range(4):
        cast_chunk(scrH1, wH1_h, c, 8, f"sm_cH1{c}")
    for c in range(4):
        cast_chunk(scrH2, wH2_h, c, 8, f"sm_cH2{c}")
    cast_chunk(scrW3, wW3_h, 0, 8, "sm_castW3")

    consts = [
        (lambda: lam_sb[:], lambda: lam_d[:]),
        (lambda: ld_sb[:], lambda: ld_d[:]),
        (lambda: w1s[:], lambda: w1s_d[:]),
        (lambda: w2s[:], lambda: w2s_d[:]),
        (lambda: h1s[:], lambda: h1s_d[:]),
        (lambda: h2s[:], lambda: h2s_d[:]),
        (lambda: w3s[:], lambda: w3s_d[:]),
        (lambda: Dv[:], lambda: Dv_d[:]),
        (lambda: idbf[:], lambda: idbf_d[:]),
        (lambda: idf[:], lambda: idf_d[:]),
        (lambda: g_scr[:, 0:KD * N].rearrange("p (k n) -> p k n", n=N),
         lambda: BT_d[:].rearrange("(k p) n -> p k n", p=128)),
    ]
    for dstf, srcf in consts:
        P.op("sync", lambda e, dstf=dstf, srcf=srcf: e.dma_start(out=dstf(), in_=srcf()),
             inc="sm_const", inc_n=16)
    consts_done = P.cnt["sm_const"]

    P.op("vector", lambda e: e.memset(ones1[:], 1.0), inc="se_dve")
    rec["ones"] = P.cnt["se_dve"]
    P.op("tensor",
         lambda e: e.matmul(out=ps_bu[0:N, 0:1], lhsT=ones1[0:1, :],
                            rhs=ld_sb[0:1, 0:1], start=True, stop=True),
         waits=[("sm_const", consts_done), ("se_dve", rec["ones"])],
         inc="se_pe")
    rec["dtb"] = P.cnt["se_pe"]
    P.op("vector",
         lambda e: e.tensor_add(out=lamdt[:], in0=lam_sb[:], in1=ps_bu[0:N, 0:1]),
         waits=[("se_pe", rec["dtb"])], inc="se_dve")
    rec["lamdt"] = P.cnt["se_dve"]
    P.op("scalar",
         lambda e: e.activation(out=e1[:], in_=lamdt[:], func=AF.Exp),
         waits=[("se_dve", rec["lamdt"])], inc="se_act")
    rec["e1"] = P.cnt["se_act"]
    P.op("scalar",
         lambda e: e.activation(out=a_sb[:], in_=e1[:], func=AF.Exp, scale=-1.0),
         waits=[("se_act", rec["e1"])], inc="se_act")
    rec["a"] = P.cnt["se_act"]
    pv = P.op("vector", lambda e: e.memset(A_rep[:], 1.0), inc="se_dve")
    pv = P.op("vector",
              lambda e: e.tensor_scalar(out=A_rep[:], in0=A_rep[:], scalar1=a_sb[:, 0:1],
                                        scalar2=None, op0=ALU.mult),
              waits=[("se_act", rec["a"]), ("se_dve", pv)], inc="se_dve")
    pv = P.op("vector", lambda e: e.memset(carry[:], 0.0),
              waits=[("se_dve", pv)], inc="se_dve")
    pv = P.op("vector", lambda e: e.tensor_mul(out=s12[:], in0=w1s[:], in1=w2s[:]),
              waits=[("sm_const", consts_done), ("se_dve", pv)], inc="se_dve")
    isq2 = 1.0 / math.sqrt(2.0)
    pv = P.op("vector", lambda e: e.tensor_scalar(out=h1sd[:], in0=h1s[:], scalar1=isq2,
                                                  scalar2=None, op0=ALU.mult),
              waits=[("se_dve", pv)], inc="se_dve")
    pv = P.op("vector", lambda e: e.tensor_scalar(out=h2sd[:], in0=h2s[:], scalar1=isq2,
                                                  scalar2=None, op0=ALU.mult),
              waits=[("se_dve", pv)], inc="se_dve")
    pv = P.op("vector", lambda e: e.tensor_copy(out=BT_sb[:], in_=g_scr[:, 0:KD * N]),
              waits=[("se_dve", pv)], inc="se_dve")
    for half in range(2):
        P.op("sync",
             lambda e, half=half: e.dma_start(
                 out=g_scr[0:N, 0:T], in_=CT_d[:, half * T:(half + 1) * T]),
             waits=[("se_dve", pv)], inc="sm_const", inc_n=16)
        ld = P.cnt["sm_const"]
        pv = P.op("vector",
                  lambda e, half=half: e.tensor_copy(
                      out=CT_sb[:, half * T:(half + 1) * T], in_=g_scr[0:N, 0:T]),
                  waits=[("sm_const", ld), ("se_dve", pv)], inc="se_dve")
    rec["consts"] = P.cnt["se_dve"]

    for g in range(RS_A):
        plan_slabA_load(g)
    for g in range(RS_B):
        plan_slabB_load(g)


    def u_sem(i, b):
        return (f"sm_u_{b}", 16 * (i + 1))

    def plan_fe_stats(i):
        par = i % 2
        pv = None
        for b in range(NB):
            uw = u_sem(i, b)
            P.op("vector",
                 lambda e, b=b, par=par: e.reduce_sum(
                     out=sums[:, b:b + 1], in_=u_tok[:, b * D:(b + 1) * D],
                     axis=AX.X),
                 waits=[uw, ("se_dve", rec["consts"] if i == 0 else None)],
                 inc="se_dve")
            pv = P.op("vector",
                      lambda e, b=b, par=par: e.scalar_tensor_tensor(
                          out=xnT[par][:].bitcast(F32)[:, 0:D],
                          in0=u_tok[:, b * D:(b + 1) * D],
                          scalar=1.0, in1=u_tok[:, b * D:(b + 1) * D],
                          op0=ALU.bypass, op1=ALU.mult, accum_out=sumsq[:, b:b + 1]),
                      waits=[uw], inc="se_dve")
        def chain(emitf):
            nonlocal pv
            pv = P.op("vector", emitf, waits=[("se_dve", pv)], inc="se_dve")

        chain(lambda e: e.tensor_scalar(
            out=mcol[:], in0=sums[:], scalar1=1.0 / D, scalar2=None, op0=ALU.mult))
        chain(lambda e: e.tensor_scalar(
            out=vpe[:], in0=sumsq[:], scalar1=1.0 / D, scalar2=EPS,
            op0=ALU.mult, op1=ALU.add))
        chain(lambda e: e.tensor_mul(out=nt0[:], in0=mcol[:], in1=mcol[:]))
        chain(lambda e: e.tensor_sub(out=vpe[:], in0=vpe[:], in1=nt0[:]))
        chain(lambda e: e.tensor_scalar(
            out=rstd[:], in0=vpe[:], scalar1=-0.5, scalar2=1.5,
            op0=ALU.mult, op1=ALU.add))
        for _ in range(4):
            chain(lambda e: e.tensor_mul(out=nt0[:], in0=rstd[:], in1=rstd[:]))
            chain(lambda e: e.tensor_mul(out=nt1[:], in0=nt0[:], in1=vpe[:]))
            chain(lambda e: e.tensor_scalar(
                out=nt0[:], in0=nt1[:], scalar1=-0.5, scalar2=1.5,
                op0=ALU.mult, op1=ALU.add))
            chain(lambda e: e.tensor_mul(out=rstd[:], in0=rstd[:], in1=nt0[:]))
        rec[("stats", i)] = P.cnt["se_dve"]
        for b in range(3):
            plan_xn(i, b)

    def plan_xn(i, b):
        par = i % 2
        waits = [u_sem(i, b), ("se_dve", rec[("stats", i)])]
        if b == 3:
            waits.append(("se_pe", rec[("xnTps", i, 0)]))
        elif i >= 1:
            waits.append(("se_pe", rec[("xnTps", i - 1, 3 if b == 0 else b)]))
        P.op("vector",
             lambda e, b=b, par=par: e.tensor_scalar(
                 out=xn_blk[b % 3][:], in0=u_tok[:, b * D:(b + 1) * D],
                 scalar1=mcol[:, b:b + 1], scalar2=rstd[:, b:b + 1],
                 op0=ALU.subtract, op1=ALU.mult),
             waits=waits, inc="se_dve")
        rec[("xn", i, b)] = P.cnt["se_dve"]

    def plan_fe_rest(i):
        par = i % 2
        for b in range(NB):
            tb = (lambda: ps_xnT[:]) if b % 2 == 0 else (lambda: ps_cs[0][:].bitcast(BF16))
            tw = [("se_dve", rec[("xn", i, b)])]
            if (i, b) not in ((0, 0), (0, 1)):
                pb = (i, b - 2) if b >= 2 else (i - 1, b + 2)
                tw.append(("se_act", rec[("xnTcp", pb[0], pb[1])]))
            for k in range(KD):
                P.op("tensor",
                     lambda e, b=b, k=k, tb=tb: e.transpose(
                         out=tb()[:, k * 128:(k + 1) * 128],
                         in_=xn_blk[b % 3][:, k * 128:(k + 1) * 128],
                         identity=idbf[:]),
                     waits=tw if k == 0 else (),
                     inc="se_pe" if k == KD - 1 else None)
            rec[("xnTps", i, b)] = P.cnt["se_pe"]
            if b == 0:
                plan_xn(i, 3)
            cw = [("se_pe", rec[("xnTps", i, b)])]
            if b == 0 and i >= 2:
                cw.append(("se_pe", rec[("Amm", i - 2, 31)]))
            P.op("scalar",
                 lambda e, b=b, par=par, tb=tb: e.copy(
                     out=xnT[par][:].rearrange("p (k t) -> p k t", t=T)[:, :, b * 128:(b + 1) * 128],
                     in_=tb().rearrange("p (k t) -> p k t", t=128)),
                 waits=cw, inc="se_act")
            rec[("xnTcp", i, b)] = P.cnt["se_act"]

        buw = [("se_act", rec[("xnTcp", i, 3)]), ("se_dve", rec["consts"])]
        if i >= 1:
            buw.append(("se_dve", rec[("scan", i - 1)]))
        for k in range(KD):
            P.op("tensor",
                 lambda e, k=k, par=par: e.matmul(
                     out=ps_bu[:], lhsT=BT_sb[:, k * N:(k + 1) * N],
                     rhs=xnT[par][:, k * T:(k + 1) * T],
                     start=(k == 0), stop=(k == KD - 1)),
                 waits=buw if k == 0 else (),
                 inc="se_pe" if k == KD - 1 else None)
        rec[("bu", i)] = P.cnt["se_pe"]

        sv = P.op("vector",
                  lambda e: e.tensor_tensor_scan(
                      out=xs[:], data0=A_rep[:], data1=ps_bu[:],
                      initial=carry[:, 0:1], op0=ALU.mult, op1=ALU.add),
                  waits=[("se_pe", rec[("bu", i)])], inc="se_dve")
        sv = P.op("vector", lambda e: e.tensor_copy(out=carry[:, 0:1], in_=xs[:, T - 1:T]),
                  waits=[("se_dve", sv)], inc="se_dve")
        P.op("vector", lambda e: e.tensor_copy(out=xs_bf[:], in_=xs[:]),
             waits=[("se_dve", sv)], inc="se_dve")
        rec[("scan", i)] = P.cnt["se_dve"]

        if debug and i == 0:
            P.op("vector", lambda e: e.tensor_copy(out=nt1[:], in_=rstd[:]),
                 inc="se_dve")
            rec["dbg_rstd_cp"] = P.cnt["se_dve"]
            P.op("sync", lambda e: e.dma_start(out=dbg["stats0"][:, 0:4], in_=sums[:]),
                 waits=[("se_dve", rec["dbg_rstd_cp"])], inc="sm_dbg", inc_n=16)
            P.op("sync", lambda e: e.dma_start(out=dbg["stats0"][:, 4:8], in_=sumsq[:]),
                 inc="sm_dbg", inc_n=16)
            P.op("sync", lambda e: e.dma_start(out=dbg["stats0"][:, 8:12], in_=nt1[:]),
                 inc="sm_dbg", inc_n=16)
            P.op("sync", lambda e: e.dma_start(out=dbg["stats0"][:, 12:16], in_=mcol[:]),
                 inc="sm_dbg", inc_n=16)
            P.op("sync", lambda e: e.dma_start(out=dbg["u0"][:], in_=u_tok[:, 0:D]),
                 inc="sm_dbg", inc_n=16)
            P.op("sync", lambda e: e.dma_start(out=dbg["xnT0"][:], in_=xnT[0][:]),
                 waits=[("se_act", rec[("xnTcp", 0, 3)])], inc="sm_dbg", inc_n=16)
            P.op("sync", lambda e: e.dma_start(out=dbg["xs0"][:], in_=xs[:]),
                 waits=[("se_dve", rec[("scan", 0)])], inc="sm_dbg", inc_n=16)

    def plan_backend_part1(i):
        par = i % 2
        if i + 1 < NT:
            for b in range(NB):
                w = [("se_dve", rec[("xn", i, 3)])] if b == 0 else []
                P.op("gpsimd",
                     lambda e, b=b, i=i: e.dma_start(
                         out=u_tok[:, b * D:(b + 1) * D],
                         in_=u_d[(i + 1) * T + b * 128:(i + 1) * T + (b + 1) * 128, :]),
                     waits=w, inc=f"sm_u_{b}", inc_n=16)

        def plan_c(j):
            w = [("se_dve", rec[("scan", i)])]
            if j >= 2:
                w.append(("se_dve", rec[("sstt", i, j - 2)]))
            elif i >= 1:
                w.append(("se_dve", rec[("final_cs", i - 1, j % 2)]))
            P.op("tensor",
                 lambda e, j=j: e.matmul(
                     out=ps_cs[j % 2][:], lhsT=CT_sb[:, j * 128:(j + 1) * 128],
                     rhs=xs_bf[:], start=True, stop=True),
                 waits=w, inc="se_pe")
            rec[("cmm", i, j)] = P.cnt["se_pe"]
            P.op("vector",
                 lambda e, j=j, par=par: e.scalar_tensor_tensor(
                     out=s_tile[:, j * T:(j + 1) * T], in0=xnT[par][:, j * T:(j + 1) * T],
                     scalar=Dv[:, j:j + 1], in1=ps_cs[j % 2][:],
                     op0=ALU.mult, op1=ALU.add),
                 waits=[("se_pe", rec[("cmm", i, j)])], inc="se_dve")
            rec[("sstt", i, j)] = P.cnt["se_dve"]

        for j in range(32):
            if j < 8 and i >= 1:
                plan_sgT(i - 1, j)
            if j >= 8 and (j - 8) % 3 == 0 and (j - 8) // 3 < KD:
                plan_c((j - 8) // 3)
            bk0, bk1 = (j % 2) * 2, (j % 2) * 2 + 1
            g = i * 32 + j
            mmw = [(f"sm_sa{g % RS_A}", 16 * (g // RS_A + 1))]
            if j >= 2:
                mmw.append(("se_dve", rec[("hstt", i, j - 2)]))
            elif i >= 1 and j == 0:
                mmw.append(("se_dve", rec[("gdve", i - 1, 7)]))
            slab = ringA[g % RS_A]
            for kc in range(KD):
                P.op("tensor",
                     lambda e, slab=slab, kc=kc, bk0=bk0, par=par: e.matmul(
                         out=ps_st[bk0][:], lhsT=slab[:, kc * 128:(kc + 1) * 128],
                         rhs=xnT[par][:, kc * T:(kc + 1) * T],
                         start=(kc == 0), stop=(kc == KD - 1)),
                     waits=mmw if kc == 0 else ())
            for kc in range(KD):
                P.op("tensor",
                     lambda e, slab=slab, kc=kc, bk1=bk1, par=par: e.matmul(
                         out=ps_st[bk1][:], lhsT=slab[:, (8 + kc) * 128:(9 + kc) * 128],
                         rhs=xnT[par][:, kc * T:(kc + 1) * T],
                         start=(kc == 0), stop=(kc == KD - 1)),
                     inc="se_pe" if kc == KD - 1 else None)
            rec[("Amm", i, j)] = P.cnt["se_pe"]
            plan_slabA_load(g + RS_A)

            aw = [("se_pe", rec[("Amm", i, j)])]
            if j >= 2:
                aw.append(("se_dve", rec[("ascr_rd", i, j - 2)]))
            elif i >= 1:
                aw.append(("se_dve", rec[("gdve", i - 1, 6 + j)]))
            P.op("scalar",
                 lambda e, j=j, bk0=bk0: e.activation(
                     out=act_scr[j % 2][:], in_=ps_st[bk0][:], func=AF.Sigmoid,
                     scale=w1s[:, j:j + 1]),
                 waits=aw, inc="se_act")
            rec[("sigA", i, j)] = P.cnt["se_act"]
            P.op("vector",
                 lambda e, j=j, bk0=bk0: e.tensor_mul(
                     out=dve_scr[j % 2][:], in0=act_scr[j % 2][:], in1=ps_st[bk0][:]),
                 waits=[("se_act", rec[("sigA", i, j)])], inc="se_dve")
            rec[("ascr_rd", i, j)] = P.cnt["se_dve"]
            hw = []
            if i >= 1 and j == 0:
                hw.append(("se_pe", rec[("w3mm", i - 1, 7)]))
            P.op("vector",
                 lambda e, j=j, bk1=bk1: e.scalar_tensor_tensor(
                     out=h_A[:, j * T:(j + 1) * T], in0=dve_scr[j % 2][:],
                     scalar=s12[:, j:j + 1], in1=ps_st[bk1][:],
                     op0=ALU.mult, op1=ALU.mult),
                 waits=hw, inc="se_dve")
            rec[("hstt", i, j)] = P.cnt["se_dve"]

        for b in range(3):
            prev_store = {0: 3, 1: 1, 2: 2}[b]
            w = [(f"sm_st{prev_store}", 16 * i)] if i >= 1 else []
            P.op("gpsimd",
                 lambda e, b=b, i=i: e.dma_start(
                     out=out_tok[:, b * D:(b + 1) * D],
                     in_=u_d[i * T + b * 128:i * T + (b + 1) * 128, :]),
                 waits=w, inc=f"sm_uo_{b}", inc_n=16)

        if i + 1 < NT:
            plan_fe_stats(i + 1)

        if debug and i == 0:
            P.op("sync", lambda e: e.dma_start(out=dbg["s0"][:], in_=s_tile[:]),
                 waits=[("se_dve", rec[("sstt", 0, 7)])], inc="sm_dbg", inc_n=16)
            P.op("sync", lambda e: e.dma_start(out=dbg["hA0"][:], in_=h_A[:]),
                 waits=[("se_dve", rec[("hstt", 0, 31)])], inc="sm_dbg", inc_n=16)

        def h_stage(name, src, dst, scale_e, scale_d, pos_base, mmrec, sttrec,
                    prev_sttrec):
            for j in range(32):
                bk = j % 4
                g = i * 72 + pos_base + j
                mmw = [(f"sm_sb{g % RS_B}", 16 * (g // RS_B + 1))]
                if j >= 4:
                    mmw.append(("se_dve", rec[(sttrec, i, j - 4)]))
                else:
                    if name == "h1":
                        mmw.append(("se_dve", rec[("ascr_rd", i, 30 + j // 2)]
                                    if j % 2 == 0 else rec[("hstt", i, 30 + j // 2)]))
                    else:
                        mmw.append(("se_dve", rec[("h1stt", i, 28 + j)]))
                slab = ringB[g % RS_B]
                for kc in range(KH):
                    kw = (mmw if kc == 0 else []) + (
                        [("se_dve", rec[(prev_sttrec, i, kc)])] if j == 0 else [])
                    P.op("tensor",
                         lambda e, slab=slab, kc=kc, bk=bk, src=src: e.matmul(
                             out=ps_st[bk][:], lhsT=slab[:, kc * 128:(kc + 1) * 128],
                             rhs=src[:, kc * T:(kc + 1) * T],
                             start=(kc == 0), stop=(kc == KH - 1)),
                         waits=kw,
                         inc="se_pe" if kc == KH - 1 else None)
                rec[(mmrec, i, j)] = P.cnt["se_pe"]
                plan_slabB_load(g + RS_B)

                aw = [("se_pe", rec[(mmrec, i, j)])]
                if j >= 2:
                    aw.append(("se_dve", rec[("ascr_rd2", name, i, j - 2)]))
                else:
                    aw.append(("se_dve", rec[("ascr_tail", name, i)]))
                P.op("scalar",
                     lambda e, j=j, bk=bk, scale_d=scale_d: e.activation(
                         out=act_scr[j % 2][:], in_=ps_st[bk][:], func=AF.Erf,
                         scale=scale_d[:, j:j + 1]),
                     waits=aw, inc="se_act")
                rec[("erf", name, i, j)] = P.cnt["se_act"]
                P.op("vector",
                     lambda e, j=j: e.tensor_scalar(
                         out=dve_scr[j % 2][:], in0=act_scr[j % 2][:], scalar1=0.5,
                         scalar2=0.5, op0=ALU.mult, op1=ALU.add),
                     waits=[("se_act", rec[("erf", name, i, j)])], inc="se_dve")
                rec[("ascr_rd2", name, i, j)] = P.cnt["se_dve"]
                P.op("vector",
                     lambda e, j=j, bk=bk, dst=dst, scale_e=scale_e: e.scalar_tensor_tensor(
                         out=dst[:, j * T:(j + 1) * T], in0=ps_st[bk][:],
                         scalar=scale_e[:, j:j + 1], in1=dve_scr[j % 2][:],
                         op0=ALU.mult, op1=ALU.mult),
                     inc="se_dve")
                rec[(sttrec, i, j)] = P.cnt["se_dve"]

        rec[("ascr_tail", "h1", i)] = rec[("ascr_rd", i, 31)]
        h_stage("h1", h_A, h_B, h1s, h1sd, 0, "h1mm", "h1stt", "hstt")
        rec[("ascr_tail", "h2", i)] = rec[("ascr_rd2", "h1", i, 31)]
        h_stage("h2", h_B, h_A, h2s, h2sd, 32, "h2mm", "h2stt", "h1stt")
        rec[("ascr_tail", "w3", i)] = rec[("ascr_rd2", "h2", i, 31)]

        if debug and i == 0:
            P.op("sync", lambda e: e.dma_start(out=dbg["hB0"][:], in_=h_B[:]),
                 waits=[("se_dve", rec[("h1stt", 0, 31)])], inc="sm_dbg", inc_n=16)

    sg_order = [0, 2, 1, 3, 4, 6, 5, 7]

    def plan_sgT(it, pos):
        if True:
            i = it
            grp = sg_order[pos]
            b, hh = grp // 2, grp % 2
            bank = pos % 2
            tw = [("se_dve", rec[("sg", i, 4 * hh + 3)])]
            if pos >= 2:
                tw.append(("se_dve", rec[("final", i, sg_order[pos - 2])]))
            elif pos == 0 and i + 1 < NT:
                tw.append(("se_act", rec[("xnTcp", i + 1, 3)]))
            for q in range(4):
                j = hh * 4 + q
                P.op("tensor",
                     lambda e, j=j, b=b, bank=bank: e.transpose(
                         out=ps_cs[bank][:, (j % 4) * 128:((j % 4) + 1) * 128],
                         in_=s_tile[:, j * T + b * 128:j * T + (b + 1) * 128],
                         identity=idf[:]),
                     waits=tw if q == 0 else (),
                     inc="se_pe" if q == 3 else None)
            rec[("sgT", i, grp)] = P.cnt["se_pe"]
            sl = b if b < 3 else 0
            fw = [("se_pe", rec[("sgT", i, grp)]), (f"sm_uo_{b}", 16 * (i + 1))]
            P.op("vector",
                 lambda e, sl=sl, hh=hh, bank=bank: e.tensor_add(
                     out=out_tok[:, sl * D + hh * T:sl * D + (hh + 1) * T],
                     in0=out_tok[:, sl * D + hh * T:sl * D + (hh + 1) * T],
                     in1=ps_cs[bank][:]),
                 waits=fw, inc="se_dve")
            rec[("final", i, grp)] = P.cnt["se_dve"]
            rec[("final_cs", i, bank)] = rec[("final", i, grp)]
            if hh == 1:
                P.op("gpsimd",
                     lambda e, sl=sl, b=b, i=i: e.dma_start(
                         out=out_d[i * T + b * 128:i * T + (b + 1) * 128, :],
                         in_=out_tok[:, sl * D:(sl + 1) * D]),
                     waits=[("se_dve", rec[("final", i, grp)])],
                     inc=f"sm_st{b}", inc_n=16)
                if b == 0:
                    P.op("gpsimd",
                         lambda e, i=i: e.dma_start(
                             out=out_tok[:, 0:D],
                             in_=u_d[i * T + 3 * 128:i * T + 4 * 128, :]),
                         waits=[("sm_st0", 16 * (i + 1))],
                         inc="sm_uo_3", inc_n=16)

    def plan_backend_part2(i):
        for j in range(KD):
            bk = j % 4
            g = i * 72 + 64 + j
            mmw = [(f"sm_sb{g % RS_B}", 16 * (g // RS_B + 1))]
            if j >= 4:
                mmw.append(("se_dve", rec[("gdve", i, j - 4)]))
            else:
                mmw.append(("se_dve", rec[("h2stt", i, 28 + j)]))
            slab = ringB[g % RS_B]
            for kc in range(KH):
                kw = (mmw if kc == 0 else []) + (
                    [("se_dve", rec[("h2stt", i, kc)])] if j == 0 else [])
                P.op("tensor",
                     lambda e, slab=slab, kc=kc, bk=bk: e.matmul(
                         out=ps_st[bk][:], lhsT=slab[:, kc * 128:(kc + 1) * 128],
                         rhs=h_A[:, kc * T:(kc + 1) * T],
                         start=(kc == 0), stop=(kc == KH - 1)),
                     waits=kw,
                     inc="se_pe" if kc == KH - 1 else None)
            rec[("w3mm", i, j)] = P.cnt["se_pe"]
            plan_slabB_load(g + RS_B)

            aw = [("se_pe", rec[("w3mm", i, j)])]
            if j >= 2:
                aw.append(("se_dve", rec[("gdve", i, j - 2)]))
            else:
                aw.append(("se_dve", rec[("ascr_tail", "w3", i)]))
            P.op("scalar",
                 lambda e, j=j, bk=bk: e.activation(
                     out=act_scr[j % 2][:], in_=ps_st[bk][:], func=AF.Sigmoid,
                     scale=w3s[:, j:j + 1]),
                 waits=aw, inc="se_act")
            rec[("sigw", i, j)] = P.cnt["se_act"]
            P.op("vector",
                 lambda e, j=j, bk=bk: e.scalar_tensor_tensor(
                     out=g_scr[:], in0=ps_st[bk][:], scalar=w3s[:, j:j + 1],
                     in1=act_scr[j % 2][:], op0=ALU.mult, op1=ALU.mult),
                 waits=[("se_act", rec[("sigw", i, j)])], inc="se_dve")
            rec[("gdve", i, j)] = P.cnt["se_dve"]
            P.op("vector",
                 lambda e, j=j: e.tensor_mul(
                     out=s_tile[:, j * T:(j + 1) * T], in0=s_tile[:, j * T:(j + 1) * T],
                     in1=g_scr[:]),
                 inc="se_dve")
            rec[("sg", i, j)] = P.cnt["se_dve"]

        if debug and i == 0:
            P.op("sync", lambda e: e.dma_start(out=dbg["sg0"][:], in_=s_tile[:]),
                 waits=[("se_dve", rec[("sg", 0, 7)])], inc="sm_dbg", inc_n=16)

    plan_fe_stats(0)
    plan_fe_rest(0)
    for i in range(NT):
        plan_backend_part1(i)
        if i + 1 < NT:
            plan_fe_rest(i + 1)
        plan_backend_part2(i)
    for pos in range(8):
        plan_sgT(NT - 1, pos)

    final_waits = [(f"sm_st{b}", 16 * NT) for b in range(NB)]
    if debug:
        final_waits.append(("sm_dbg", P.cnt["sm_dbg"]))
    P.op("gpsimd", lambda e: e.engine_nop(), waits=final_waits)

    with nc.Block() as block:
        @block.sync
        def _(e):
            for f in P.prog["sync"]:
                f(e)

        @block.gpsimd
        def _(e):
            for f in P.prog["gpsimd"]:
                f(e)

        @block.tensor
        def _(e):
            for f in P.prog["tensor"]:
                f(e)

        @block.vector
        def _(e):
            for f in P.prog["vector"]:
                f(e)

        @block.scalar
        def _(e):
            for f in P.prog["scalar"]:
                f(e)

    return nc, sbuf_left


def _layout_inputs(inputs):
    f32 = np.float32
    u = np.ascontiguousarray(np.asarray(inputs["L0_attn_u"], dtype=f32))

    def slabs(w, nj, nkc):
        w = np.asarray(w, dtype=f32).reshape(nj, 128, nkc, 128)
        return w.transpose(0, 3, 2, 1)

    wA = np.ascontiguousarray(np.concatenate(
        [slabs(inputs["L0_ffn_w1_w"], 32, 8), slabs(inputs["L0_ffn_w2_w"], 32, 8)],
        axis=2))
    wH1 = np.ascontiguousarray(slabs(inputs["L0_ffn_h1_w"], 32, 32))
    wH2 = np.ascontiguousarray(slabs(inputs["L0_ffn_h2_w"], 32, 32))
    wW3 = np.ascontiguousarray(slabs(inputs["L0_ffn_w3_w"], 8, 32))
    BT = np.ascontiguousarray(np.asarray(inputs["L0_attn_B"], dtype=f32).T)
    CT = np.ascontiguousarray(np.asarray(inputs["L0_attn_C"], dtype=f32).T)

    def colmaj(v, nj):
        return np.ascontiguousarray(np.asarray(v, dtype=f32).reshape(nj, 128).T)

    common = {
        "wA": wA, "wH1": wH1, "wH2": wH2, "wW3": wW3,
        "BT": BT, "CT": CT,
        "w1s": colmaj(inputs["L0_ffn_w1_s"], 32),
        "w2s": colmaj(inputs["L0_ffn_w2_s"], 32),
        "h1s": colmaj(inputs["L0_ffn_h1_s"], 32),
        "h2s": colmaj(inputs["L0_ffn_h2_s"], 32),
        "w3s": colmaj(inputs["L0_ffn_w3_s"], 8),
        "Dv": colmaj(inputs["L0_attn_D"], 8),
        "Lam": np.asarray(inputs["L0_attn_Lambda"], dtype=f32).reshape(N, 1),
        "logdt": np.asarray(inputs["L0_attn_log_dt"], dtype=f32).reshape(1, 1),
        "idbf": np.eye(128, dtype=ml_dtypes.bfloat16),
        "idf": np.eye(128, dtype=f32),
    }
    in_maps = []
    for b in range(8):
        m = dict(common)
        m["u"] = np.ascontiguousarray(u[b])
        in_maps.append(m)
    return in_maps


def kernel(**inputs):
    in_maps = _layout_inputs(inputs)
    nc, _ = build_program(debug=DEBUG)
    res = run_bass_kernel_spmd(nc, in_maps, core_ids=list(range(8)))
    out = np.stack([r["out"] for r in res.results], axis=0)
    if DEBUG:
        kernel._dbg = res.results
    return out


# version 12
# speedup vs baseline: 1.1352x; 1.1352x over previous
import math
import numpy as np
import ml_dtypes
import concourse.bass as bass
import concourse.mybir as mybir
from concourse.bass_utils import run_bass_kernel_spmd

F32 = mybir.dt.float32
BF16 = mybir.dt.bfloat16
AF = mybir.ActivationFunctionType
ALU = mybir.AluOpType
AX = mybir.AxisListType

L, D, N, H = 8192, 1024, 64, 4096
T = 512
NT = L // T
NB = T // 128
KD = D // 128
KH = H // 128
RS_A = 4
RS_B = 4
EPS = 1e-5

DEBUG = False


class Plan:

    def __init__(self, nc):
        self.nc = nc
        self.prog = {k: [] for k in ("sync", "gpsimd", "tensor", "vector", "scalar")}
        self.cnt = {}
        self.waited = {}
        self.sems = {}

    def sem(self, name):
        if name not in self.sems:
            self.sems[name] = self.nc.alloc_semaphore(name)
            self.cnt[name] = 0
        return self.sems[name]

    def bump(self, name, k=1):
        self.cnt[name] += k
        return self.cnt[name]

    def op(self, eng, emit, waits=(), inc=None, inc_n=1):
        sems = self.sems
        real_waits = []
        for sname, val in waits:
            if val is None or val <= 0:
                continue
            key = (eng, sname)
            if self.waited.get(key, 0) >= val:
                continue
            self.waited[key] = val
            real_waits.append((sname, val))
        rec = None
        if inc is not None:
            rec = self.bump(inc, inc_n)

        def f(e, real_waits=real_waits, emit=emit, incname=inc, inc_amt=inc_n):
            for sname, val in real_waits:
                e.wait_ge(sems[sname], val)
            ins = emit(e)
            if incname is not None:
                ins.then_inc(sems[incname], inc_amt)

        self.prog[eng].append(f)
        return rec


def build_program(debug=False):
    nc = bass.Bass("TRN2", target_bir_lowering=False)

    u_d = nc.dram_tensor("u", [L, D], F32, kind="ExternalInput")
    wA_h = nc.dram_tensor("wA", [32, 128, 16, 128], F32, kind="ExternalInput")
    wH1_h = nc.dram_tensor("wH1", [32, 128, 32, 128], F32, kind="ExternalInput")
    wH2_h = nc.dram_tensor("wH2", [32, 128, 32, 128], F32, kind="ExternalInput")
    wW3_h = nc.dram_tensor("wW3", [8, 128, 32, 128], F32, kind="ExternalInput")
    BT_d = nc.dram_tensor("BT", [D, N], F32, kind="ExternalInput")
    CT_d = nc.dram_tensor("CT", [N, D], F32, kind="ExternalInput")
    w1s_d = nc.dram_tensor("w1s", [128, 32], F32, kind="ExternalInput")
    w2s_d = nc.dram_tensor("w2s", [128, 32], F32, kind="ExternalInput")
    h1s_d = nc.dram_tensor("h1s", [128, 32], F32, kind="ExternalInput")
    h2s_d = nc.dram_tensor("h2s", [128, 32], F32, kind="ExternalInput")
    w3s_d = nc.dram_tensor("w3s", [128, 8], F32, kind="ExternalInput")
    Dv_d = nc.dram_tensor("Dv", [128, 8], F32, kind="ExternalInput")
    lam_d = nc.dram_tensor("Lam", [N, 1], F32, kind="ExternalInput")
    ld_d = nc.dram_tensor("logdt", [1, 1], F32, kind="ExternalInput")
    idbf_d = nc.dram_tensor("idbf", [128, 128], BF16, kind="ExternalInput")
    idf_d = nc.dram_tensor("idf", [128, 128], F32, kind="ExternalInput")
    out_d = nc.dram_tensor("out", [L, D], F32, kind="ExternalOutput")

    scrA = nc.dram_tensor("scrA", [32, 128, 16, 128], BF16, kind="Internal")
    scrH1 = nc.dram_tensor("scrH1", [32, 128, 32, 128], BF16, kind="Internal")
    scrH2 = nc.dram_tensor("scrH2", [32, 128, 32, 128], BF16, kind="Internal")
    scrW3 = nc.dram_tensor("scrW3", [8, 128, 32, 128], BF16, kind="Internal")

    dbg = {}
    if debug:
        dbg["xnT0"] = nc.dram_tensor("dbg_xnT0", [128, 4096], BF16, kind="ExternalOutput")
        dbg["xs0"] = nc.dram_tensor("dbg_xs0", [N, T], F32, kind="ExternalOutput")
        dbg["s0"] = nc.dram_tensor("dbg_s0", [128, 4096], F32, kind="ExternalOutput")
        dbg["hA0"] = nc.dram_tensor("dbg_hA0", [128, 16384], BF16, kind="ExternalOutput")
        dbg["hB0"] = nc.dram_tensor("dbg_hB0", [128, 16384], BF16, kind="ExternalOutput")
        dbg["sg0"] = nc.dram_tensor("dbg_sg0", [128, 4096], F32, kind="ExternalOutput")
        dbg["stats0"] = nc.dram_tensor("dbg_stats0", [128, 16], F32, kind="ExternalOutput")
        dbg["u0"] = nc.dram_tensor("dbg_u0", [128, 1024], F32, kind="ExternalOutput")

    A = nc.alloc_sbuf_tensor
    u_tok = A("u_tok", [128, NB * D], F32)
    xnT = [A(f"xnT{p}", [128, KD * T], BF16) for p in range(2)]
    xn_blk = [A(f"xn_blk{p}", [128, D], BF16) for p in range(3)]
    h_A = A("h_A", [128, KH * T], BF16)
    h_B = A("h_B", [128, KH * T], BF16)
    s_tile = A("s_tile", [128, KD * T], F32)
    out_tok = A("out_tok", [128, 3 * D], F32)
    ringA = [A(f"ringA{r}", [128, 16 * 128], BF16) for r in range(RS_A)]
    ringB = [A(f"ringB{r}", [128, 32 * 128], BF16) for r in range(RS_B)]
    act_scr = [A(f"act_scr{p}", [128, T], F32) for p in range(2)]
    dve_scr = [A(f"dve_scr{p}", [128, T], F32) for p in range(2)]
    g_scr = A("g_scr", [128, T], F32)
    A_rep = A("A_rep", [N, T], F32)
    xs = A("xs", [N, T], F32)
    xs_bf = A("xs_bf", [N, T], BF16)
    carry = A("carry", [N, 1], F32)
    a_sb = A("a_sb", [N, 1], F32)
    lam_sb = A("lam_sb", [N, 1], F32)
    ld_sb = A("ld_sb", [1, 1], F32)
    lamdt = A("lamdt", [N, 1], F32)
    e1 = A("e1", [N, 1], F32)
    ones1 = A("ones1", [1, N], F32)
    BT_sb = A("BT_sb", [128, KD * N], BF16)
    CT_sb = A("CT_sb", [N, D], BF16)
    w1s = A("w1s_sb", [128, 32], F32)
    w2s = A("w2s_sb", [128, 32], F32)
    h1s = A("h1s_sb", [128, 32], F32)
    h2s = A("h2s_sb", [128, 32], F32)
    w3s = A("w3s_sb", [128, 8], F32)
    Dv = A("Dv_sb", [128, 8], F32)
    s12 = A("s12_sb", [128, 32], F32)
    h1sd = A("h1sd_sb", [128, 32], F32)
    h2sd = A("h2sd_sb", [128, 32], F32)
    idbf = A("idbf_sb", [128, 128], BF16)
    idf = A("idf_sb", [128, 128], F32)
    sums = A("sums", [128, NB], F32)
    sumsq = A("sumsq", [128, NB], F32)
    mcol = A("mcol", [128, NB], F32)
    vpe = A("vpe", [128, NB], F32)
    rstd = A("rstd", [128, NB], F32)
    nmr = A("nmr", [128, NB], F32)
    nt0 = A("nt0", [128, NB], F32)
    nt1 = A("nt1", [128, NB], F32)

    ps_st = [nc.alloc_psum_tensor(f"ps_st{k}", [128, T], F32) for k in range(4)]
    ps_bu = nc.alloc_psum_tensor("ps_bu", [N, T], F32)
    ps_xnT = nc.alloc_psum_tensor("ps_xnT", [128, 1024], BF16)
    ps_cs = [nc.alloc_psum_tensor(f"ps_cs{k}", [128, T], F32) for k in range(2)]

    sbuf_left = nc.sbuf_bytes_remaining

    P = Plan(nc)
    for s in (["sm_castW3", "sm_const", "se_pe", "se_act", "se_dve", "sm_dbg"]
              + [f"sm_cA{c}" for c in range(4)]
              + [f"sm_cH1{c}" for c in range(4)]
              + [f"sm_cH2{c}" for c in range(4)]
              + [f"sm_u_{b}" for b in range(NB)]
              + [f"sm_uo_{b}" for b in range(NB)]
              + [f"sm_sa{r}" for r in range(RS_A)]
              + [f"sm_sb{r}" for r in range(RS_B)]
              + [f"sm_st{b}" for b in range(NB)]):
        P.sem(s)

    rec = {}

    def plan_slabA_load(g):
        if g >= NT * 32:
            return
        i, j = g // 32, g % 32
        waits = []
        if i == 0:
            waits.append((f"sm_cA{j // 8}", 16))
        if g >= RS_A:
            gp = g - RS_A
            waits.append(("se_pe", rec[("Amm", gp // 32, gp % 32)]))
        P.op("sync",
             lambda e, j=j, g=g: e.dma_start(
                 out=ringA[g % RS_A][:],
                 in_=scrA[j].rearrange("p s c -> p (s c)")),
             waits=waits, inc=f"sm_sa{g % RS_A}", inc_n=16)

    def big_rec(g):
        i, pos = g // 72, g % 72
        if pos < 32:
            return rec[("h1mm", i, pos)]
        if pos < 64:
            return rec[("h2mm", i, pos - 32)]
        return rec[("w3mm", i, pos - 64)]

    def plan_slabB_load(g):
        if g >= NT * 72:
            return
        i, pos = g // 72, g % 72
        if pos < 32:
            scr, j, cast, castv = scrH1, pos, f"sm_cH1{pos // 8}", 8 * 16
        elif pos < 64:
            scr, j, cast, castv = scrH2, pos - 32, f"sm_cH2{(pos - 32) // 8}", 8 * 16
        else:
            scr, j, cast, castv = scrW3, pos - 64, "sm_castW3", 8 * 16
        waits = []
        if i == 0:
            waits.append((cast, 16))
        if g >= RS_B:
            waits.append(("se_pe", big_rec(g - RS_B)))
        P.op("sync",
             lambda e, scr=scr, j=j, g=g: e.dma_start(
                 out=ringB[g % RS_B][:],
                 in_=scr[j].rearrange("p kc c -> p (kc c)")),
             waits=waits, inc=f"sm_sb{g % RS_B}", inc_n=16)

    for b in range(NB):
        P.op("sync",
             lambda e, b=b: e.dma_start(
                 out=u_tok[:, b * D:(b + 1) * D],
                 in_=u_d[b * 128:(b + 1) * 128, :]),
             inc=f"sm_u_{b}", inc_n=16)

    def cast_chunk(dst, src, c, nj, sem):
        P.op("gpsimd",
             lambda e, dst=dst, src=src, c=c, nj=nj: e.dma_start(
                 out=dst[c * nj:(c + 1) * nj].rearrange("j p s c -> (j p) (s c)"),
                 in_=src[c * nj:(c + 1) * nj].rearrange("j p s c -> (j p) (s c)")),
             inc=sem, inc_n=16)

    for c in range(4):
        cast_chunk(scrA, wA_h, c, 8, f"sm_cA{c}")
    for c in range(4):
        cast_chunk(scrH1, wH1_h, c, 8, f"sm_cH1{c}")
    for c in range(4):
        cast_chunk(scrH2, wH2_h, c, 8, f"sm_cH2{c}")
    cast_chunk(scrW3, wW3_h, 0, 8, "sm_castW3")

    consts = [
        (lambda: lam_sb[:], lambda: lam_d[:]),
        (lambda: ld_sb[:], lambda: ld_d[:]),
        (lambda: w1s[:], lambda: w1s_d[:]),
        (lambda: w2s[:], lambda: w2s_d[:]),
        (lambda: h1s[:], lambda: h1s_d[:]),
        (lambda: h2s[:], lambda: h2s_d[:]),
        (lambda: w3s[:], lambda: w3s_d[:]),
        (lambda: Dv[:], lambda: Dv_d[:]),
        (lambda: idbf[:], lambda: idbf_d[:]),
        (lambda: idf[:], lambda: idf_d[:]),
        (lambda: g_scr[:, 0:KD * N].rearrange("p (k n) -> p k n", n=N),
         lambda: BT_d[:].rearrange("(k p) n -> p k n", p=128)),
    ]
    for dstf, srcf in consts:
        P.op("sync", lambda e, dstf=dstf, srcf=srcf: e.dma_start(out=dstf(), in_=srcf()),
             inc="sm_const", inc_n=16)
    consts_done = P.cnt["sm_const"]

    P.op("vector", lambda e: e.memset(ones1[:], 1.0), inc="se_dve")
    rec["ones"] = P.cnt["se_dve"]
    P.op("tensor",
         lambda e: e.matmul(out=ps_bu[0:N, 0:1], lhsT=ones1[0:1, :],
                            rhs=ld_sb[0:1, 0:1], start=True, stop=True),
         waits=[("sm_const", consts_done), ("se_dve", rec["ones"])],
         inc="se_pe")
    rec["dtb"] = P.cnt["se_pe"]
    P.op("vector",
         lambda e: e.tensor_add(out=lamdt[:], in0=lam_sb[:], in1=ps_bu[0:N, 0:1]),
         waits=[("se_pe", rec["dtb"])], inc="se_dve")
    rec["lamdt"] = P.cnt["se_dve"]
    P.op("scalar",
         lambda e: e.activation(out=e1[:], in_=lamdt[:], func=AF.Exp),
         waits=[("se_dve", rec["lamdt"])], inc="se_act")
    rec["e1"] = P.cnt["se_act"]
    P.op("scalar",
         lambda e: e.activation(out=a_sb[:], in_=e1[:], func=AF.Exp, scale=-1.0),
         waits=[("se_act", rec["e1"])], inc="se_act")
    rec["a"] = P.cnt["se_act"]
    pv = P.op("vector", lambda e: e.memset(A_rep[:], 1.0), inc="se_dve")
    pv = P.op("vector",
              lambda e: e.tensor_scalar(out=A_rep[:], in0=A_rep[:], scalar1=a_sb[:, 0:1],
                                        scalar2=None, op0=ALU.mult),
              waits=[("se_act", rec["a"]), ("se_dve", pv)], inc="se_dve")
    pv = P.op("vector", lambda e: e.memset(carry[:], 0.0),
              waits=[("se_dve", pv)], inc="se_dve")
    pv = P.op("vector", lambda e: e.tensor_mul(out=s12[:], in0=w1s[:], in1=w2s[:]),
              waits=[("sm_const", consts_done), ("se_dve", pv)], inc="se_dve")
    isq2 = 1.0 / math.sqrt(2.0)
    pv = P.op("vector", lambda e: e.tensor_scalar(out=h1sd[:], in0=h1s[:], scalar1=isq2,
                                                  scalar2=None, op0=ALU.mult),
              waits=[("se_dve", pv)], inc="se_dve")
    pv = P.op("vector", lambda e: e.tensor_scalar(out=h2sd[:], in0=h2s[:], scalar1=isq2,
                                                  scalar2=None, op0=ALU.mult),
              waits=[("se_dve", pv)], inc="se_dve")
    pv = P.op("vector", lambda e: e.tensor_copy(out=BT_sb[:], in_=g_scr[:, 0:KD * N]),
              waits=[("se_dve", pv)], inc="se_dve")
    for half in range(2):
        P.op("sync",
             lambda e, half=half: e.dma_start(
                 out=g_scr[0:N, 0:T], in_=CT_d[:, half * T:(half + 1) * T]),
             waits=[("se_dve", pv)], inc="sm_const", inc_n=16)
        ld = P.cnt["sm_const"]
        pv = P.op("vector",
                  lambda e, half=half: e.tensor_copy(
                      out=CT_sb[:, half * T:(half + 1) * T], in_=g_scr[0:N, 0:T]),
                  waits=[("sm_const", ld), ("se_dve", pv)], inc="se_dve")
    rec["consts"] = P.cnt["se_dve"]

    for g in range(RS_A):
        plan_slabA_load(g)
    for g in range(RS_B):
        plan_slabB_load(g)


    def u_sem(i, b):
        return (f"sm_u_{b}", 16 * (i + 1))

    def plan_fe_stats(i):
        par = i % 2
        pv = None
        for b in range(NB):
            uw = u_sem(i, b)
            P.op("vector",
                 lambda e, b=b, par=par: e.reduce_sum(
                     out=sums[:, b:b + 1], in_=u_tok[:, b * D:(b + 1) * D],
                     axis=AX.X),
                 waits=[uw, ("se_dve", rec["consts"] if i == 0 else None)],
                 inc="se_dve")
            pv = P.op("vector",
                      lambda e, b=b, par=par: e.scalar_tensor_tensor(
                          out=xnT[par][:].bitcast(F32)[:, 0:D],
                          in0=u_tok[:, b * D:(b + 1) * D],
                          scalar=1.0, in1=u_tok[:, b * D:(b + 1) * D],
                          op0=ALU.bypass, op1=ALU.mult, accum_out=sumsq[:, b:b + 1]),
                      waits=[uw], inc="se_dve")
        def chain(emitf):
            nonlocal pv
            pv = P.op("vector", emitf, waits=[("se_dve", pv)], inc="se_dve")

        chain(lambda e: e.tensor_scalar(
            out=mcol[:], in0=sums[:], scalar1=1.0 / D, scalar2=None, op0=ALU.mult))
        chain(lambda e: e.tensor_scalar(
            out=vpe[:], in0=sumsq[:], scalar1=1.0 / D, scalar2=EPS,
            op0=ALU.mult, op1=ALU.add))
        chain(lambda e: e.tensor_mul(out=nt0[:], in0=mcol[:], in1=mcol[:]))
        chain(lambda e: e.tensor_sub(out=vpe[:], in0=vpe[:], in1=nt0[:]))
        chain(lambda e: e.tensor_scalar(
            out=rstd[:], in0=vpe[:], scalar1=-0.5, scalar2=1.5,
            op0=ALU.mult, op1=ALU.add))
        for _ in range(4):
            chain(lambda e: e.tensor_mul(out=nt0[:], in0=rstd[:], in1=rstd[:]))
            chain(lambda e: e.tensor_mul(out=nt1[:], in0=nt0[:], in1=vpe[:]))
            chain(lambda e: e.tensor_scalar(
                out=nt0[:], in0=nt1[:], scalar1=-0.5, scalar2=1.5,
                op0=ALU.mult, op1=ALU.add))
            chain(lambda e: e.tensor_mul(out=rstd[:], in0=rstd[:], in1=nt0[:]))
        rec[("stats", i)] = P.cnt["se_dve"]
        for b in range(3):
            plan_xn(i, b)

    def plan_xn(i, b):
        par = i % 2
        waits = [u_sem(i, b), ("se_dve", rec[("stats", i)])]
        if b == 3:
            waits.append(("se_pe", rec[("xnTps", i, 0)]))
        elif i >= 1:
            waits.append(("se_pe", rec[("xnTps", i - 1, 3 if b == 0 else b)]))
        P.op("vector",
             lambda e, b=b, par=par: e.tensor_scalar(
                 out=xn_blk[b % 3][:], in0=u_tok[:, b * D:(b + 1) * D],
                 scalar1=mcol[:, b:b + 1], scalar2=rstd[:, b:b + 1],
                 op0=ALU.subtract, op1=ALU.mult),
             waits=waits, inc="se_dve")
        rec[("xn", i, b)] = P.cnt["se_dve"]

    def plan_fe_block(i, b):
        par = i % 2
        if True:
            tb = (lambda: ps_xnT[:]) if b % 2 == 0 else (lambda: ps_cs[0][:].bitcast(BF16))
            tw = [("se_dve", rec[("xn", i, b)])]
            if (i, b) not in ((0, 0), (0, 1)):
                pb = (i, b - 2) if b >= 2 else (i - 1, b + 2)
                tw.append(("se_act", rec[("xnTcp", pb[0], pb[1])]))
            for k in range(KD):
                P.op("tensor",
                     lambda e, b=b, k=k, tb=tb: e.transpose(
                         out=tb()[:, k * 128:(k + 1) * 128],
                         in_=xn_blk[b % 3][:, k * 128:(k + 1) * 128],
                         identity=idbf[:]),
                     waits=tw if k == 0 else (),
                     inc="se_pe" if k == KD - 1 else None)
            rec[("xnTps", i, b)] = P.cnt["se_pe"]
            if b == 0:
                plan_xn(i, 3)
            cw = [("se_pe", rec[("xnTps", i, b)])]
            if b == 0 and i >= 2:
                cw.append(("se_pe", rec[("Amm", i - 2, 31)]))
            P.op("scalar",
                 lambda e, b=b, par=par, tb=tb: e.copy(
                     out=xnT[par][:].rearrange("p (k t) -> p k t", t=T)[:, :, b * 128:(b + 1) * 128],
                     in_=tb().rearrange("p (k t) -> p k t", t=128)),
                 waits=cw, inc="se_act")
            rec[("xnTcp", i, b)] = P.cnt["se_act"]

    def plan_fe_rest(i):
        par = i % 2
        if i == 0:
            for b in range(NB):
                plan_fe_block(0, b)
        buw = [("se_act", rec[("xnTcp", i, 3)]), ("se_dve", rec["consts"])]
        if i >= 1:
            buw.append(("se_dve", rec[("scan", i - 1)]))
        for k in range(KD):
            P.op("tensor",
                 lambda e, k=k, par=par: e.matmul(
                     out=ps_bu[:], lhsT=BT_sb[:, k * N:(k + 1) * N],
                     rhs=xnT[par][:, k * T:(k + 1) * T],
                     start=(k == 0), stop=(k == KD - 1)),
                 waits=buw if k == 0 else (),
                 inc="se_pe" if k == KD - 1 else None)
        rec[("bu", i)] = P.cnt["se_pe"]

        sv = P.op("vector",
                  lambda e: e.tensor_tensor_scan(
                      out=xs[:], data0=A_rep[:], data1=ps_bu[:],
                      initial=carry[:, 0:1], op0=ALU.mult, op1=ALU.add),
                  waits=[("se_pe", rec[("bu", i)])], inc="se_dve")
        sv = P.op("vector", lambda e: e.tensor_copy(out=carry[:, 0:1], in_=xs[:, T - 1:T]),
                  waits=[("se_dve", sv)], inc="se_dve")
        P.op("vector", lambda e: e.tensor_copy(out=xs_bf[:], in_=xs[:]),
             waits=[("se_dve", sv)], inc="se_dve")
        rec[("scan", i)] = P.cnt["se_dve"]

        if debug and i == 0:
            P.op("vector", lambda e: e.tensor_copy(out=nt1[:], in_=rstd[:]),
                 inc="se_dve")
            rec["dbg_rstd_cp"] = P.cnt["se_dve"]
            P.op("sync", lambda e: e.dma_start(out=dbg["stats0"][:, 0:4], in_=sums[:]),
                 waits=[("se_dve", rec["dbg_rstd_cp"])], inc="sm_dbg", inc_n=16)
            P.op("sync", lambda e: e.dma_start(out=dbg["stats0"][:, 4:8], in_=sumsq[:]),
                 inc="sm_dbg", inc_n=16)
            P.op("sync", lambda e: e.dma_start(out=dbg["stats0"][:, 8:12], in_=nt1[:]),
                 inc="sm_dbg", inc_n=16)
            P.op("sync", lambda e: e.dma_start(out=dbg["stats0"][:, 12:16], in_=mcol[:]),
                 inc="sm_dbg", inc_n=16)
            P.op("sync", lambda e: e.dma_start(out=dbg["u0"][:], in_=u_tok[:, 0:D]),
                 inc="sm_dbg", inc_n=16)
            P.op("sync", lambda e: e.dma_start(out=dbg["xnT0"][:], in_=xnT[0][:]),
                 waits=[("se_act", rec[("xnTcp", 0, 3)])], inc="sm_dbg", inc_n=16)
            P.op("sync", lambda e: e.dma_start(out=dbg["xs0"][:], in_=xs[:]),
                 waits=[("se_dve", rec[("scan", 0)])], inc="sm_dbg", inc_n=16)

    def plan_backend_part1(i):
        par = i % 2
        if i + 1 < NT:
            for b in range(NB):
                w = [("se_dve", rec[("xn", i, 3)])] if b == 0 else []
                P.op("gpsimd",
                     lambda e, b=b, i=i: e.dma_start(
                         out=u_tok[:, b * D:(b + 1) * D],
                         in_=u_d[(i + 1) * T + b * 128:(i + 1) * T + (b + 1) * 128, :]),
                     waits=w, inc=f"sm_u_{b}", inc_n=16)

        def plan_c(j):
            w = [("se_dve", rec[("scan", i)])]
            if j >= 2:
                w.append(("se_dve", rec[("sstt", i, j - 2)]))
            elif i >= 1:
                w.append(("se_dve", rec[("final_cs", i - 1, j % 2)]))
            P.op("tensor",
                 lambda e, j=j: e.matmul(
                     out=ps_cs[j % 2][:], lhsT=CT_sb[:, j * 128:(j + 1) * 128],
                     rhs=xs_bf[:], start=True, stop=True),
                 waits=w, inc="se_pe")
            rec[("cmm", i, j)] = P.cnt["se_pe"]
            P.op("vector",
                 lambda e, j=j, par=par: e.scalar_tensor_tensor(
                     out=s_tile[:, j * T:(j + 1) * T], in0=xnT[par][:, j * T:(j + 1) * T],
                     scalar=Dv[:, j:j + 1], in1=ps_cs[j % 2][:],
                     op0=ALU.mult, op1=ALU.add),
                 waits=[("se_pe", rec[("cmm", i, j)])], inc="se_dve")
            rec[("sstt", i, j)] = P.cnt["se_dve"]

        for j in range(32):
            if j < 8 and i >= 1:
                plan_sgT(i - 1, j)
            if j >= 8 and (j - 8) % 3 == 0 and (j - 8) // 3 < KD:
                plan_c((j - 8) // 3)
            bk0, bk1 = (j % 2) * 2, (j % 2) * 2 + 1
            g = i * 32 + j
            mmw = [(f"sm_sa{g % RS_A}", 16 * (g // RS_A + 1))]
            if j >= 2:
                mmw.append(("se_dve", rec[("hstt", i, j - 2)]))
            elif i >= 1 and j == 0:
                mmw.append(("se_dve", rec[("gdve", i - 1, 7)]))
            slab = ringA[g % RS_A]
            for kc in range(KD):
                P.op("tensor",
                     lambda e, slab=slab, kc=kc, bk0=bk0, par=par: e.matmul(
                         out=ps_st[bk0][:], lhsT=slab[:, kc * 128:(kc + 1) * 128],
                         rhs=xnT[par][:, kc * T:(kc + 1) * T],
                         start=(kc == 0), stop=(kc == KD - 1)),
                     waits=mmw if kc == 0 else ())
            for kc in range(KD):
                P.op("tensor",
                     lambda e, slab=slab, kc=kc, bk1=bk1, par=par: e.matmul(
                         out=ps_st[bk1][:], lhsT=slab[:, (8 + kc) * 128:(9 + kc) * 128],
                         rhs=xnT[par][:, kc * T:(kc + 1) * T],
                         start=(kc == 0), stop=(kc == KD - 1)),
                     inc="se_pe" if kc == KD - 1 else None)
            rec[("Amm", i, j)] = P.cnt["se_pe"]
            plan_slabA_load(g + RS_A)

            aw = [("se_pe", rec[("Amm", i, j)])]
            if j >= 2:
                aw.append(("se_dve", rec[("ascr_rd", i, j - 2)]))
            elif i >= 1:
                aw.append(("se_dve", rec[("gdve", i - 1, 6 + j)]))
            P.op("scalar",
                 lambda e, j=j, bk0=bk0: e.activation(
                     out=act_scr[j % 2][:], in_=ps_st[bk0][:], func=AF.Sigmoid,
                     scale=w1s[:, j:j + 1]),
                 waits=aw, inc="se_act")
            rec[("sigA", i, j)] = P.cnt["se_act"]
            P.op("vector",
                 lambda e, j=j, bk0=bk0: e.tensor_mul(
                     out=dve_scr[j % 2][:], in0=act_scr[j % 2][:], in1=ps_st[bk0][:]),
                 waits=[("se_act", rec[("sigA", i, j)])], inc="se_dve")
            rec[("ascr_rd", i, j)] = P.cnt["se_dve"]
            hw = []
            if i >= 1 and j == 0:
                hw.append(("se_pe", rec[("w3mm", i - 1, 7)]))
            P.op("vector",
                 lambda e, j=j, bk1=bk1: e.scalar_tensor_tensor(
                     out=h_A[:, j * T:(j + 1) * T], in0=dve_scr[j % 2][:],
                     scalar=s12[:, j:j + 1], in1=ps_st[bk1][:],
                     op0=ALU.mult, op1=ALU.mult),
                 waits=hw, inc="se_dve")
            rec[("hstt", i, j)] = P.cnt["se_dve"]

        for b in range(3):
            prev_store = {0: 3, 1: 1, 2: 2}[b]
            w = [(f"sm_st{prev_store}", 16 * i)] if i >= 1 else []
            P.op("gpsimd",
                 lambda e, b=b, i=i: e.dma_start(
                     out=out_tok[:, b * D:(b + 1) * D],
                     in_=u_d[i * T + b * 128:i * T + (b + 1) * 128, :]),
                 waits=w, inc=f"sm_uo_{b}", inc_n=16)

        if i + 1 < NT:
            plan_fe_stats(i + 1)

        if debug and i == 0:
            P.op("sync", lambda e: e.dma_start(out=dbg["s0"][:], in_=s_tile[:]),
                 waits=[("se_dve", rec[("sstt", 0, 7)])], inc="sm_dbg", inc_n=16)
            P.op("sync", lambda e: e.dma_start(out=dbg["hA0"][:], in_=h_A[:]),
                 waits=[("se_dve", rec[("hstt", 0, 31)])], inc="sm_dbg", inc_n=16)

        def h_stage(name, src, dst, scale_e, scale_d, pos_base, mmrec, sttrec,
                    prev_sttrec, cb=None):
            for j in range(32):
                if cb is not None:
                    cb(j)
                bk = j % 4
                g = i * 72 + pos_base + j
                mmw = [(f"sm_sb{g % RS_B}", 16 * (g // RS_B + 1))]
                if j >= 4:
                    mmw.append(("se_dve", rec[(sttrec, i, j - 4)]))
                else:
                    if name == "h1":
                        mmw.append(("se_dve", rec[("ascr_rd", i, 30 + j // 2)]
                                    if j % 2 == 0 else rec[("hstt", i, 30 + j // 2)]))
                    else:
                        mmw.append(("se_dve", rec[("h1stt", i, 28 + j)]))
                slab = ringB[g % RS_B]
                for kc in range(KH):
                    kw = (mmw if kc == 0 else []) + (
                        [("se_dve", rec[(prev_sttrec, i, kc)])] if j == 0 else [])
                    P.op("tensor",
                         lambda e, slab=slab, kc=kc, bk=bk, src=src: e.matmul(
                             out=ps_st[bk][:], lhsT=slab[:, kc * 128:(kc + 1) * 128],
                             rhs=src[:, kc * T:(kc + 1) * T],
                             start=(kc == 0), stop=(kc == KH - 1)),
                         waits=kw,
                         inc="se_pe" if kc == KH - 1 else None)
                rec[(mmrec, i, j)] = P.cnt["se_pe"]
                plan_slabB_load(g + RS_B)

                aw = [("se_pe", rec[(mmrec, i, j)])]
                if j >= 2:
                    aw.append(("se_dve", rec[("ascr_rd2", name, i, j - 2)]))
                else:
                    aw.append(("se_dve", rec[("ascr_tail", name, i)]))
                P.op("scalar",
                     lambda e, j=j, bk=bk, scale_d=scale_d: e.activation(
                         out=act_scr[j % 2][:], in_=ps_st[bk][:], func=AF.Erf,
                         scale=scale_d[:, j:j + 1]),
                     waits=aw, inc="se_act")
                rec[("erf", name, i, j)] = P.cnt["se_act"]
                P.op("vector",
                     lambda e, j=j: e.tensor_scalar(
                         out=dve_scr[j % 2][:], in0=act_scr[j % 2][:], scalar1=0.5,
                         scalar2=0.5, op0=ALU.mult, op1=ALU.add),
                     waits=[("se_act", rec[("erf", name, i, j)])], inc="se_dve")
                rec[("ascr_rd2", name, i, j)] = P.cnt["se_dve"]
                P.op("vector",
                     lambda e, j=j, bk=bk, dst=dst, scale_e=scale_e: e.scalar_tensor_tensor(
                         out=dst[:, j * T:(j + 1) * T], in0=ps_st[bk][:],
                         scalar=scale_e[:, j:j + 1], in1=dve_scr[j % 2][:],
                         op0=ALU.mult, op1=ALU.mult),
                     inc="se_dve")
                rec[(sttrec, i, j)] = P.cnt["se_dve"]

        rec[("ascr_tail", "h1", i)] = rec[("ascr_rd", i, 31)]
        h_stage("h1", h_A, h_B, h1s, h1sd, 0, "h1mm", "h1stt", "hstt")
        rec[("ascr_tail", "h2", i)] = rec[("ascr_rd2", "h1", i, 31)]
        def h2_cb(j):
            if i + 1 < NT and j in (24, 26, 28, 30):
                plan_fe_block(i + 1, (j - 24) // 2)

        h_stage("h2", h_B, h_A, h2s, h2sd, 32, "h2mm", "h2stt", "h1stt",
                cb=h2_cb)
        rec[("ascr_tail", "w3", i)] = rec[("ascr_rd2", "h2", i, 31)]

        if debug and i == 0:
            P.op("sync", lambda e: e.dma_start(out=dbg["hB0"][:], in_=h_B[:]),
                 waits=[("se_dve", rec[("h1stt", 0, 31)])], inc="sm_dbg", inc_n=16)

    sg_order = [0, 2, 1, 3, 4, 6, 5, 7]

    def plan_sgT(it, pos):
        if True:
            i = it
            grp = sg_order[pos]
            b, hh = grp // 2, grp % 2
            bank = pos % 2
            tw = [("se_dve", rec[("sg", i, 4 * hh + 3)])]
            if pos >= 2:
                tw.append(("se_dve", rec[("final", i, sg_order[pos - 2])]))
            elif pos == 0 and i + 1 < NT:
                tw.append(("se_act", rec[("xnTcp", i + 1, 3)]))
            for q in range(4):
                j = hh * 4 + q
                P.op("tensor",
                     lambda e, j=j, b=b, bank=bank: e.transpose(
                         out=ps_cs[bank][:, (j % 4) * 128:((j % 4) + 1) * 128],
                         in_=s_tile[:, j * T + b * 128:j * T + (b + 1) * 128],
                         identity=idf[:]),
                     waits=tw if q == 0 else (),
                     inc="se_pe" if q == 3 else None)
            rec[("sgT", i, grp)] = P.cnt["se_pe"]
            sl = b if b < 3 else 0
            fw = [("se_pe", rec[("sgT", i, grp)]), (f"sm_uo_{b}", 16 * (i + 1))]
            P.op("vector",
                 lambda e, sl=sl, hh=hh, bank=bank: e.tensor_add(
                     out=out_tok[:, sl * D + hh * T:sl * D + (hh + 1) * T],
                     in0=out_tok[:, sl * D + hh * T:sl * D + (hh + 1) * T],
                     in1=ps_cs[bank][:]),
                 waits=fw, inc="se_dve")
            rec[("final", i, grp)] = P.cnt["se_dve"]
            rec[("final_cs", i, bank)] = rec[("final", i, grp)]
            if hh == 1:
                P.op("gpsimd",
                     lambda e, sl=sl, b=b, i=i: e.dma_start(
                         out=out_d[i * T + b * 128:i * T + (b + 1) * 128, :],
                         in_=out_tok[:, sl * D:(sl + 1) * D]),
                     waits=[("se_dve", rec[("final", i, grp)])],
                     inc=f"sm_st{b}", inc_n=16)
                if b == 0:
                    P.op("gpsimd",
                         lambda e, i=i: e.dma_start(
                             out=out_tok[:, 0:D],
                             in_=u_d[i * T + 3 * 128:i * T + 4 * 128, :]),
                         waits=[("sm_st0", 16 * (i + 1))],
                         inc="sm_uo_3", inc_n=16)

    def plan_backend_part2(i):
        for j in range(KD):
            bk = j % 4
            g = i * 72 + 64 + j
            mmw = [(f"sm_sb{g % RS_B}", 16 * (g // RS_B + 1))]
            if j >= 4:
                mmw.append(("se_dve", rec[("gdve", i, j - 4)]))
            else:
                mmw.append(("se_dve", rec[("h2stt", i, 28 + j)]))
            slab = ringB[g % RS_B]
            for kc in range(KH):
                kw = (mmw if kc == 0 else []) + (
                    [("se_dve", rec[("h2stt", i, kc)])] if j == 0 else [])
                P.op("tensor",
                     lambda e, slab=slab, kc=kc, bk=bk: e.matmul(
                         out=ps_st[bk][:], lhsT=slab[:, kc * 128:(kc + 1) * 128],
                         rhs=h_A[:, kc * T:(kc + 1) * T],
                         start=(kc == 0), stop=(kc == KH - 1)),
                     waits=kw,
                     inc="se_pe" if kc == KH - 1 else None)
            rec[("w3mm", i, j)] = P.cnt["se_pe"]
            plan_slabB_load(g + RS_B)

            aw = [("se_pe", rec[("w3mm", i, j)])]
            if j >= 2:
                aw.append(("se_dve", rec[("gdve", i, j - 2)]))
            else:
                aw.append(("se_dve", rec[("ascr_tail", "w3", i)]))
            P.op("scalar",
                 lambda e, j=j, bk=bk: e.activation(
                     out=act_scr[j % 2][:], in_=ps_st[bk][:], func=AF.Sigmoid,
                     scale=w3s[:, j:j + 1]),
                 waits=aw, inc="se_act")
            rec[("sigw", i, j)] = P.cnt["se_act"]
            P.op("vector",
                 lambda e, j=j, bk=bk: e.scalar_tensor_tensor(
                     out=g_scr[:], in0=ps_st[bk][:], scalar=w3s[:, j:j + 1],
                     in1=act_scr[j % 2][:], op0=ALU.mult, op1=ALU.mult),
                 waits=[("se_act", rec[("sigw", i, j)])], inc="se_dve")
            rec[("gdve", i, j)] = P.cnt["se_dve"]
            P.op("vector",
                 lambda e, j=j: e.tensor_mul(
                     out=s_tile[:, j * T:(j + 1) * T], in0=s_tile[:, j * T:(j + 1) * T],
                     in1=g_scr[:]),
                 inc="se_dve")
            rec[("sg", i, j)] = P.cnt["se_dve"]

        if debug and i == 0:
            P.op("sync", lambda e: e.dma_start(out=dbg["sg0"][:], in_=s_tile[:]),
                 waits=[("se_dve", rec[("sg", 0, 7)])], inc="sm_dbg", inc_n=16)

    plan_fe_stats(0)
    plan_fe_rest(0)
    for i in range(NT):
        plan_backend_part1(i)
        if i + 1 < NT:
            plan_fe_rest(i + 1)
        plan_backend_part2(i)
    for pos in range(8):
        plan_sgT(NT - 1, pos)

    final_waits = [(f"sm_st{b}", 16 * NT) for b in range(NB)]
    if debug:
        final_waits.append(("sm_dbg", P.cnt["sm_dbg"]))
    P.op("gpsimd", lambda e: e.engine_nop(), waits=final_waits)

    with nc.Block() as block:
        @block.sync
        def _(e):
            for f in P.prog["sync"]:
                f(e)

        @block.gpsimd
        def _(e):
            for f in P.prog["gpsimd"]:
                f(e)

        @block.tensor
        def _(e):
            for f in P.prog["tensor"]:
                f(e)

        @block.vector
        def _(e):
            for f in P.prog["vector"]:
                f(e)

        @block.scalar
        def _(e):
            for f in P.prog["scalar"]:
                f(e)

    return nc, sbuf_left


def _layout_inputs(inputs):
    f32 = np.float32
    u = np.ascontiguousarray(np.asarray(inputs["L0_attn_u"], dtype=f32))

    def slabs(w, nj, nkc):
        w = np.asarray(w, dtype=f32).reshape(nj, 128, nkc, 128)
        return w.transpose(0, 3, 2, 1)

    wA = np.ascontiguousarray(np.concatenate(
        [slabs(inputs["L0_ffn_w1_w"], 32, 8), slabs(inputs["L0_ffn_w2_w"], 32, 8)],
        axis=2))
    wH1 = np.ascontiguousarray(slabs(inputs["L0_ffn_h1_w"], 32, 32))
    wH2 = np.ascontiguousarray(slabs(inputs["L0_ffn_h2_w"], 32, 32))
    wW3 = np.ascontiguousarray(slabs(inputs["L0_ffn_w3_w"], 8, 32))
    BT = np.ascontiguousarray(np.asarray(inputs["L0_attn_B"], dtype=f32).T)
    CT = np.ascontiguousarray(np.asarray(inputs["L0_attn_C"], dtype=f32).T)

    def colmaj(v, nj):
        return np.ascontiguousarray(np.asarray(v, dtype=f32).reshape(nj, 128).T)

    common = {
        "wA": wA, "wH1": wH1, "wH2": wH2, "wW3": wW3,
        "BT": BT, "CT": CT,
        "w1s": colmaj(inputs["L0_ffn_w1_s"], 32),
        "w2s": colmaj(inputs["L0_ffn_w2_s"], 32),
        "h1s": colmaj(inputs["L0_ffn_h1_s"], 32),
        "h2s": colmaj(inputs["L0_ffn_h2_s"], 32),
        "w3s": colmaj(inputs["L0_ffn_w3_s"], 8),
        "Dv": colmaj(inputs["L0_attn_D"], 8),
        "Lam": np.asarray(inputs["L0_attn_Lambda"], dtype=f32).reshape(N, 1),
        "logdt": np.asarray(inputs["L0_attn_log_dt"], dtype=f32).reshape(1, 1),
        "idbf": np.eye(128, dtype=ml_dtypes.bfloat16),
        "idf": np.eye(128, dtype=f32),
    }
    in_maps = []
    for b in range(8):
        m = dict(common)
        m["u"] = np.ascontiguousarray(u[b])
        in_maps.append(m)
    return in_maps


def kernel(**inputs):
    in_maps = _layout_inputs(inputs)
    nc, _ = build_program(debug=DEBUG)
    res = run_bass_kernel_spmd(nc, in_maps, core_ids=list(range(8)))
    out = np.stack([r["out"] for r in res.results], axis=0)
    if DEBUG:
        kernel._dbg = res.results
    return out


# version 13
# speedup vs baseline: 1.1926x; 1.0506x over previous
import math
import numpy as np
import ml_dtypes
import concourse.bass as bass
import concourse.mybir as mybir
from concourse.bass_utils import run_bass_kernel_spmd

F32 = mybir.dt.float32
BF16 = mybir.dt.bfloat16
AF = mybir.ActivationFunctionType
ALU = mybir.AluOpType
AX = mybir.AxisListType

L, D, N, H = 8192, 1024, 64, 4096
T = 512
NT = L // T
NB = T // 128
KD = D // 128
KH = H // 128
RS_A = 4
RS_B = 4
EPS = 1e-5

DEBUG = False


class Plan:

    def __init__(self, nc):
        self.nc = nc
        self.prog = {k: [] for k in ("sync", "gpsimd", "tensor", "vector", "scalar")}
        self.cnt = {}
        self.waited = {}
        self.sems = {}

    def sem(self, name):
        if name not in self.sems:
            self.sems[name] = self.nc.alloc_semaphore(name)
            self.cnt[name] = 0
        return self.sems[name]

    def bump(self, name, k=1):
        self.cnt[name] += k
        return self.cnt[name]

    def op(self, eng, emit, waits=(), inc=None, inc_n=1):
        sems = self.sems
        real_waits = []
        for sname, val in waits:
            if val is None or val <= 0:
                continue
            key = (eng, sname)
            if self.waited.get(key, 0) >= val:
                continue
            self.waited[key] = val
            real_waits.append((sname, val))
        rec = None
        if inc is not None:
            rec = self.bump(inc, inc_n)

        def f(e, real_waits=real_waits, emit=emit, incname=inc, inc_amt=inc_n):
            for sname, val in real_waits:
                e.wait_ge(sems[sname], val)
            ins = emit(e)
            if incname is not None:
                ins.then_inc(sems[incname], inc_amt)

        self.prog[eng].append(f)
        return rec


def build_program(debug=False):
    nc = bass.Bass("TRN2", target_bir_lowering=False)

    u_d = nc.dram_tensor("u", [L, D], F32, kind="ExternalInput")
    wA_h = nc.dram_tensor("wA", [32, 128, 16, 128], F32, kind="ExternalInput")
    wH1_h = nc.dram_tensor("wH1", [32, 128, 32, 128], F32, kind="ExternalInput")
    wH2_h = nc.dram_tensor("wH2", [32, 128, 32, 128], F32, kind="ExternalInput")
    wW3_h = nc.dram_tensor("wW3", [8, 128, 32, 128], F32, kind="ExternalInput")
    BT_d = nc.dram_tensor("BT", [D, N], F32, kind="ExternalInput")
    CT_d = nc.dram_tensor("CT", [N, D], F32, kind="ExternalInput")
    w1s_d = nc.dram_tensor("w1s", [128, 32], F32, kind="ExternalInput")
    w2s_d = nc.dram_tensor("w2s", [128, 32], F32, kind="ExternalInput")
    h1s_d = nc.dram_tensor("h1s", [128, 32], F32, kind="ExternalInput")
    h2s_d = nc.dram_tensor("h2s", [128, 32], F32, kind="ExternalInput")
    w3s_d = nc.dram_tensor("w3s", [128, 8], F32, kind="ExternalInput")
    Dv_d = nc.dram_tensor("Dv", [128, 8], F32, kind="ExternalInput")
    lam_d = nc.dram_tensor("Lam", [N, 1], F32, kind="ExternalInput")
    ld_d = nc.dram_tensor("logdt", [1, 1], F32, kind="ExternalInput")
    idbf_d = nc.dram_tensor("idbf", [128, 128], BF16, kind="ExternalInput")
    idf_d = nc.dram_tensor("idf", [128, 128], F32, kind="ExternalInput")
    out_d = nc.dram_tensor("out", [L, D], F32, kind="ExternalOutput")

    scrA = nc.dram_tensor("scrA", [32, 128, 16, 128], BF16, kind="Internal")
    scrH1 = nc.dram_tensor("scrH1", [32, 128, 32, 128], BF16, kind="Internal")
    scrH2 = nc.dram_tensor("scrH2", [32, 128, 32, 128], BF16, kind="Internal")
    scrW3 = nc.dram_tensor("scrW3", [8, 128, 32, 128], BF16, kind="Internal")

    dbg = {}
    if debug:
        dbg["xnT0"] = nc.dram_tensor("dbg_xnT0", [128, 4096], BF16, kind="ExternalOutput")
        dbg["xs0"] = nc.dram_tensor("dbg_xs0", [N, T], F32, kind="ExternalOutput")
        dbg["s0"] = nc.dram_tensor("dbg_s0", [128, 4096], F32, kind="ExternalOutput")
        dbg["hA0"] = nc.dram_tensor("dbg_hA0", [128, 16384], BF16, kind="ExternalOutput")
        dbg["hB0"] = nc.dram_tensor("dbg_hB0", [128, 16384], BF16, kind="ExternalOutput")
        dbg["sg0"] = nc.dram_tensor("dbg_sg0", [128, 4096], F32, kind="ExternalOutput")
        dbg["stats0"] = nc.dram_tensor("dbg_stats0", [128, 16], F32, kind="ExternalOutput")
        dbg["u0"] = nc.dram_tensor("dbg_u0", [128, 1024], F32, kind="ExternalOutput")

    A = nc.alloc_sbuf_tensor
    u_tok = A("u_tok", [128, NB * D], F32)
    xnT = [A(f"xnT{p}", [128, KD * T], BF16) for p in range(2)]
    xn_blk = [A(f"xn_blk{p}", [128, D], BF16) for p in range(3)]
    h_A = A("h_A", [128, KH * T], BF16)
    h_B = A("h_B", [128, KH * T], BF16)
    s_tile = A("s_tile", [128, KD * T], F32)
    out_tok = A("out_tok", [128, 3 * D], F32)
    ringA = [A(f"ringA{r}", [128, 16 * 128], BF16) for r in range(RS_A)]
    ringB = [A(f"ringB{r}", [128, 32 * 128], BF16) for r in range(RS_B)]
    act_scr = [A(f"act_scr{p}", [128, T], F32) for p in range(2)]
    dve_scr = [A(f"dve_scr{p}", [128, T], F32) for p in range(2)]
    g_scr = A("g_scr", [128, T], F32)
    A_rep = A("A_rep", [N, T], F32)
    xs = A("xs", [N, T], F32)
    xs_bf = A("xs_bf", [N, T], BF16)
    carry = A("carry", [N, 1], F32)
    a_sb = A("a_sb", [N, 1], F32)
    lam_sb = A("lam_sb", [N, 1], F32)
    ld_sb = A("ld_sb", [1, 1], F32)
    lamdt = A("lamdt", [N, 1], F32)
    e1 = A("e1", [N, 1], F32)
    ones1 = A("ones1", [1, N], F32)
    BT_sb = A("BT_sb", [128, KD * N], BF16)
    CT_sb = A("CT_sb", [N, D], BF16)
    w1s = A("w1s_sb", [128, 32], F32)
    w2s = A("w2s_sb", [128, 32], F32)
    h1s = A("h1s_sb", [128, 32], F32)
    h2s = A("h2s_sb", [128, 32], F32)
    w3s = A("w3s_sb", [128, 8], F32)
    Dv = A("Dv_sb", [128, 8], F32)
    s12 = A("s12_sb", [128, 32], F32)
    h1sd = A("h1sd_sb", [128, 32], F32)
    h2sd = A("h2sd_sb", [128, 32], F32)
    idbf = A("idbf_sb", [128, 128], BF16)
    idf = A("idf_sb", [128, 128], F32)
    sums = A("sums", [128, NB], F32)
    sumsq = A("sumsq", [128, NB], F32)
    mcol = A("mcol", [128, NB], F32)
    vpe = A("vpe", [128, NB], F32)
    rstd = A("rstd", [128, NB], F32)
    nmr = A("nmr", [128, NB], F32)
    nt0 = A("nt0", [128, NB], F32)
    nt1 = A("nt1", [128, NB], F32)

    ps_st = [nc.alloc_psum_tensor(f"ps_st{k}", [128, T], F32) for k in range(4)]
    ps_bu = nc.alloc_psum_tensor("ps_bu", [N, T], F32)
    ps_xnT = nc.alloc_psum_tensor("ps_xnT", [128, 1024], BF16)
    ps_cs = [nc.alloc_psum_tensor(f"ps_cs{k}", [128, T], F32) for k in range(2)]

    sbuf_left = nc.sbuf_bytes_remaining

    P = Plan(nc)
    for s in (["sm_castW3", "sm_const", "se_pe", "se_act", "se_dve", "sm_dbg"]
              + [f"sm_cA{c}" for c in range(4)]
              + [f"sm_cH1{c}" for c in range(4)]
              + [f"sm_cH2{c}" for c in range(4)]
              + [f"sm_u_{b}" for b in range(NB)]
              + [f"sm_uo_{b}" for b in range(NB)]
              + [f"sm_sa{r}" for r in range(RS_A)]
              + [f"sm_sb{r}" for r in range(RS_B)]
              + [f"sm_st{b}" for b in range(NB)]):
        P.sem(s)

    rec = {}

    def plan_slabA_load(g):
        if g >= NT * 32:
            return
        i, j = g // 32, g % 32
        waits = []
        if i == 0:
            waits.append((f"sm_cA{j // 8}", 16))
        if g >= RS_A:
            gp = g - RS_A
            waits.append(("se_pe", rec[("Amm", gp // 32, gp % 32)]))
        P.op("sync",
             lambda e, j=j, g=g: e.dma_start(
                 out=ringA[g % RS_A][:],
                 in_=scrA[j].rearrange("p s c -> p (s c)")),
             waits=waits, inc=f"sm_sa{g % RS_A}", inc_n=16)

    def big_rec(g):
        i, pos = g // 72, g % 72
        if pos < 32:
            return rec[("h1mm", i, pos)]
        if pos < 64:
            return rec[("h2mm", i, pos - 32)]
        return rec[("w3mm", i, pos - 64)]

    def plan_slabB_load(g):
        if g >= NT * 72:
            return
        i, pos = g // 72, g % 72
        if pos < 32:
            scr, j, cast, castv = scrH1, pos, f"sm_cH1{pos // 8}", 8 * 16
        elif pos < 64:
            scr, j, cast, castv = scrH2, pos - 32, f"sm_cH2{(pos - 32) // 8}", 8 * 16
        else:
            scr, j, cast, castv = scrW3, pos - 64, "sm_castW3", 8 * 16
        waits = []
        if i == 0:
            waits.append((cast, 16))
        if g >= RS_B:
            waits.append(("se_pe", big_rec(g - RS_B)))
        P.op("sync",
             lambda e, scr=scr, j=j, g=g: e.dma_start(
                 out=ringB[g % RS_B][:],
                 in_=scr[j].rearrange("p kc c -> p (kc c)")),
             waits=waits, inc=f"sm_sb{g % RS_B}", inc_n=16)

    for b in range(NB):
        P.op("sync",
             lambda e, b=b: e.dma_start(
                 out=u_tok[:, b * D:(b + 1) * D],
                 in_=u_d[b * 128:(b + 1) * 128, :]),
             inc=f"sm_u_{b}", inc_n=16)

    def cast_chunk(dst, src, c, nj, sem):
        P.op("gpsimd",
             lambda e, dst=dst, src=src, c=c, nj=nj: e.dma_start(
                 out=dst[c * nj:(c + 1) * nj].rearrange("j p s c -> (j p) (s c)"),
                 in_=src[c * nj:(c + 1) * nj].rearrange("j p s c -> (j p) (s c)")),
             inc=sem, inc_n=16)

    for c in range(4):
        cast_chunk(scrA, wA_h, c, 8, f"sm_cA{c}")
    for c in range(4):
        cast_chunk(scrH1, wH1_h, c, 8, f"sm_cH1{c}")
    for c in range(4):
        cast_chunk(scrH2, wH2_h, c, 8, f"sm_cH2{c}")
    cast_chunk(scrW3, wW3_h, 0, 8, "sm_castW3")

    consts = [
        (lambda: lam_sb[:], lambda: lam_d[:]),
        (lambda: ld_sb[:], lambda: ld_d[:]),
        (lambda: w1s[:], lambda: w1s_d[:]),
        (lambda: w2s[:], lambda: w2s_d[:]),
        (lambda: h1s[:], lambda: h1s_d[:]),
        (lambda: h2s[:], lambda: h2s_d[:]),
        (lambda: w3s[:], lambda: w3s_d[:]),
        (lambda: Dv[:], lambda: Dv_d[:]),
        (lambda: idbf[:], lambda: idbf_d[:]),
        (lambda: idf[:], lambda: idf_d[:]),
        (lambda: g_scr[:, 0:KD * N].rearrange("p (k n) -> p k n", n=N),
         lambda: BT_d[:].rearrange("(k p) n -> p k n", p=128)),
    ]
    for dstf, srcf in consts:
        P.op("sync", lambda e, dstf=dstf, srcf=srcf: e.dma_start(out=dstf(), in_=srcf()),
             inc="sm_const", inc_n=16)
    consts_done = P.cnt["sm_const"]

    def plan_prologue_compute():
        plan_prologue_compute_body()

    def plan_prologue_compute_body():
        P.op("vector", lambda e: e.memset(ones1[:], 1.0), inc="se_dve")
        rec["ones"] = P.cnt["se_dve"]
        P.op("tensor",
             lambda e: e.matmul(out=ps_bu[0:N, 0:1], lhsT=ones1[0:1, :],
                                rhs=ld_sb[0:1, 0:1], start=True, stop=True),
             waits=[("sm_const", consts_done), ("se_dve", rec["ones"])],
             inc="se_pe")
        rec["dtb"] = P.cnt["se_pe"]
        P.op("vector",
             lambda e: e.tensor_add(out=lamdt[:], in0=lam_sb[:], in1=ps_bu[0:N, 0:1]),
             waits=[("se_pe", rec["dtb"])], inc="se_dve")
        rec["lamdt"] = P.cnt["se_dve"]
        P.op("scalar",
             lambda e: e.activation(out=e1[:], in_=lamdt[:], func=AF.Exp),
             waits=[("se_dve", rec["lamdt"])], inc="se_act")
        rec["e1"] = P.cnt["se_act"]
        P.op("scalar",
             lambda e: e.activation(out=a_sb[:], in_=e1[:], func=AF.Exp, scale=-1.0),
             waits=[("se_act", rec["e1"])], inc="se_act")
        rec["a"] = P.cnt["se_act"]
        pv = P.op("vector", lambda e: e.memset(A_rep[:], 1.0), inc="se_dve")
        pv = P.op("vector",
                  lambda e: e.tensor_scalar(out=A_rep[:], in0=A_rep[:], scalar1=a_sb[:, 0:1],
                                            scalar2=None, op0=ALU.mult),
                  waits=[("se_act", rec["a"]), ("se_dve", pv)], inc="se_dve")
        pv = P.op("vector", lambda e: e.memset(carry[:], 0.0),
                  waits=[("se_dve", pv)], inc="se_dve")
        pv = P.op("vector", lambda e: e.tensor_mul(out=s12[:], in0=w1s[:], in1=w2s[:]),
                  waits=[("sm_const", consts_done), ("se_dve", pv)], inc="se_dve")
        isq2 = 1.0 / math.sqrt(2.0)
        pv = P.op("vector", lambda e: e.tensor_scalar(out=h1sd[:], in0=h1s[:], scalar1=isq2,
                                                      scalar2=None, op0=ALU.mult),
                  waits=[("se_dve", pv)], inc="se_dve")
        pv = P.op("vector", lambda e: e.tensor_scalar(out=h2sd[:], in0=h2s[:], scalar1=isq2,
                                                      scalar2=None, op0=ALU.mult),
                  waits=[("se_dve", pv)], inc="se_dve")
        pv = P.op("vector", lambda e: e.tensor_copy(out=BT_sb[:], in_=g_scr[:, 0:KD * N]),
                  waits=[("se_dve", pv)], inc="se_dve")
        for half in range(2):
            P.op("sync",
                 lambda e, half=half: e.dma_start(
                     out=g_scr[0:N, 0:T], in_=CT_d[:, half * T:(half + 1) * T]),
                 waits=[("se_dve", pv)], inc="sm_const", inc_n=16)
            ld = P.cnt["sm_const"]
            pv = P.op("vector",
                      lambda e, half=half: e.tensor_copy(
                          out=CT_sb[:, half * T:(half + 1) * T], in_=g_scr[0:N, 0:T]),
                      waits=[("sm_const", ld), ("se_dve", pv)], inc="se_dve")
        rec["consts"] = P.cnt["se_dve"]

        for g in range(RS_A):
            plan_slabA_load(g)
        for g in range(RS_B):
            plan_slabB_load(g)


    def u_sem(i, b):
        return (f"sm_u_{b}", 16 * (i + 1))

    def plan_fe_stats(i):
        par = i % 2
        pv = None
        for b in range(NB):
            uw = u_sem(i, b)
            P.op("vector",
                 lambda e, b=b, par=par: e.reduce_sum(
                     out=sums[:, b:b + 1], in_=u_tok[:, b * D:(b + 1) * D],
                     axis=AX.X),
                 waits=[uw], inc="se_dve")
            pv = P.op("vector",
                      lambda e, b=b, par=par: e.scalar_tensor_tensor(
                          out=xnT[par][:].bitcast(F32)[:, 0:D],
                          in0=u_tok[:, b * D:(b + 1) * D],
                          scalar=1.0, in1=u_tok[:, b * D:(b + 1) * D],
                          op0=ALU.bypass, op1=ALU.mult, accum_out=sumsq[:, b:b + 1]),
                      waits=[uw], inc="se_dve")
        def chain(emitf):
            nonlocal pv
            pv = P.op("vector", emitf, waits=[("se_dve", pv)], inc="se_dve")

        chain(lambda e: e.tensor_scalar(
            out=mcol[:], in0=sums[:], scalar1=1.0 / D, scalar2=None, op0=ALU.mult))
        chain(lambda e: e.tensor_scalar(
            out=vpe[:], in0=sumsq[:], scalar1=1.0 / D, scalar2=EPS,
            op0=ALU.mult, op1=ALU.add))
        chain(lambda e: e.tensor_mul(out=nt0[:], in0=mcol[:], in1=mcol[:]))
        chain(lambda e: e.tensor_sub(out=vpe[:], in0=vpe[:], in1=nt0[:]))
        chain(lambda e: e.tensor_scalar(
            out=rstd[:], in0=vpe[:], scalar1=-0.5, scalar2=1.5,
            op0=ALU.mult, op1=ALU.add))
        for _ in range(4):
            chain(lambda e: e.tensor_mul(out=nt0[:], in0=rstd[:], in1=rstd[:]))
            chain(lambda e: e.tensor_mul(out=nt1[:], in0=nt0[:], in1=vpe[:]))
            chain(lambda e: e.tensor_scalar(
                out=nt0[:], in0=nt1[:], scalar1=-0.5, scalar2=1.5,
                op0=ALU.mult, op1=ALU.add))
            chain(lambda e: e.tensor_mul(out=rstd[:], in0=rstd[:], in1=nt0[:]))
        rec[("stats", i)] = P.cnt["se_dve"]
        for b in range(3):
            plan_xn(i, b)

    def plan_xn(i, b):
        par = i % 2
        waits = [u_sem(i, b), ("se_dve", rec[("stats", i)])]
        if b == 3:
            waits.append(("se_pe", rec[("xnTps", i, 0)]))
        elif i >= 1:
            waits.append(("se_pe", rec[("xnTps", i - 1, 3 if b == 0 else b)]))
        P.op("vector",
             lambda e, b=b, par=par: e.tensor_scalar(
                 out=xn_blk[b % 3][:], in0=u_tok[:, b * D:(b + 1) * D],
                 scalar1=mcol[:, b:b + 1], scalar2=rstd[:, b:b + 1],
                 op0=ALU.subtract, op1=ALU.mult),
             waits=waits, inc="se_dve")
        rec[("xn", i, b)] = P.cnt["se_dve"]

    def plan_fe_block(i, b):
        par = i % 2
        if True:
            tb = (lambda: ps_xnT[:]) if b % 2 == 0 else (lambda: ps_cs[0][:].bitcast(BF16))
            tw = [("se_dve", rec[("xn", i, b)])]
            if (i, b) not in ((0, 0), (0, 1)):
                pb = (i, b - 2) if b >= 2 else (i - 1, b + 2)
                tw.append(("se_act", rec[("xnTcp", pb[0], pb[1])]))
            for k in range(KD):
                P.op("tensor",
                     lambda e, b=b, k=k, tb=tb: e.transpose(
                         out=tb()[:, k * 128:(k + 1) * 128],
                         in_=xn_blk[b % 3][:, k * 128:(k + 1) * 128],
                         identity=idbf[:]),
                     waits=tw if k == 0 else (),
                     inc="se_pe" if k == KD - 1 else None)
            rec[("xnTps", i, b)] = P.cnt["se_pe"]
            if b == 0:
                plan_xn(i, 3)
            cw = [("se_pe", rec[("xnTps", i, b)])]
            if b == 0 and i >= 2:
                cw.append(("se_pe", rec[("Amm", i - 2, 31)]))
            P.op("scalar",
                 lambda e, b=b, par=par, tb=tb: e.copy(
                     out=xnT[par][:].rearrange("p (k t) -> p k t", t=T)[:, :, b * 128:(b + 1) * 128],
                     in_=tb().rearrange("p (k t) -> p k t", t=128)),
                 waits=cw, inc="se_act")
            rec[("xnTcp", i, b)] = P.cnt["se_act"]

    def plan_fe_rest(i):
        par = i % 2
        if i == 0:
            for b in range(NB):
                plan_fe_block(0, b)
        buw = [("se_act", rec[("xnTcp", i, 3)]), ("se_dve", rec["consts"])]
        if i >= 1:
            buw.append(("se_dve", rec[("scan", i - 1)]))
        for k in range(KD):
            P.op("tensor",
                 lambda e, k=k, par=par: e.matmul(
                     out=ps_bu[:], lhsT=BT_sb[:, k * N:(k + 1) * N],
                     rhs=xnT[par][:, k * T:(k + 1) * T],
                     start=(k == 0), stop=(k == KD - 1)),
                 waits=buw if k == 0 else (),
                 inc="se_pe" if k == KD - 1 else None)
        rec[("bu", i)] = P.cnt["se_pe"]

        sv = P.op("vector",
                  lambda e: e.tensor_tensor_scan(
                      out=xs[:], data0=A_rep[:], data1=ps_bu[:],
                      initial=carry[:, 0:1], op0=ALU.mult, op1=ALU.add),
                  waits=[("se_pe", rec[("bu", i)])], inc="se_dve")
        sv = P.op("vector", lambda e: e.tensor_copy(out=carry[:, 0:1], in_=xs[:, T - 1:T]),
                  waits=[("se_dve", sv)], inc="se_dve")
        P.op("vector", lambda e: e.tensor_copy(out=xs_bf[:], in_=xs[:]),
             waits=[("se_dve", sv)], inc="se_dve")
        rec[("scan", i)] = P.cnt["se_dve"]

        if debug and i == 0:
            P.op("vector", lambda e: e.tensor_copy(out=nt1[:], in_=rstd[:]),
                 inc="se_dve")
            rec["dbg_rstd_cp"] = P.cnt["se_dve"]
            P.op("sync", lambda e: e.dma_start(out=dbg["stats0"][:, 0:4], in_=sums[:]),
                 waits=[("se_dve", rec["dbg_rstd_cp"])], inc="sm_dbg", inc_n=16)
            P.op("sync", lambda e: e.dma_start(out=dbg["stats0"][:, 4:8], in_=sumsq[:]),
                 inc="sm_dbg", inc_n=16)
            P.op("sync", lambda e: e.dma_start(out=dbg["stats0"][:, 8:12], in_=nt1[:]),
                 inc="sm_dbg", inc_n=16)
            P.op("sync", lambda e: e.dma_start(out=dbg["stats0"][:, 12:16], in_=mcol[:]),
                 inc="sm_dbg", inc_n=16)
            P.op("sync", lambda e: e.dma_start(out=dbg["u0"][:], in_=u_tok[:, 0:D]),
                 inc="sm_dbg", inc_n=16)
            P.op("sync", lambda e: e.dma_start(out=dbg["xnT0"][:], in_=xnT[0][:]),
                 waits=[("se_act", rec[("xnTcp", 0, 3)])], inc="sm_dbg", inc_n=16)
            P.op("sync", lambda e: e.dma_start(out=dbg["xs0"][:], in_=xs[:]),
                 waits=[("se_dve", rec[("scan", 0)])], inc="sm_dbg", inc_n=16)

    def plan_backend_part1(i):
        par = i % 2
        if i + 1 < NT:
            for b in range(NB):
                w = [("se_dve", rec[("xn", i, 3)])] if b == 0 else []
                P.op("gpsimd",
                     lambda e, b=b, i=i: e.dma_start(
                         out=u_tok[:, b * D:(b + 1) * D],
                         in_=u_d[(i + 1) * T + b * 128:(i + 1) * T + (b + 1) * 128, :]),
                     waits=w, inc=f"sm_u_{b}", inc_n=16)

        def plan_c(j):
            w = [("se_dve", rec[("scan", i)])]
            if j >= 2:
                w.append(("se_dve", rec[("sstt", i, j - 2)]))
            elif i >= 1:
                w.append(("se_dve", rec[("final_cs", i - 1, j % 2)]))
            P.op("tensor",
                 lambda e, j=j: e.matmul(
                     out=ps_cs[j % 2][:], lhsT=CT_sb[:, j * 128:(j + 1) * 128],
                     rhs=xs_bf[:], start=True, stop=True),
                 waits=w, inc="se_pe")
            rec[("cmm", i, j)] = P.cnt["se_pe"]
            P.op("vector",
                 lambda e, j=j, par=par: e.scalar_tensor_tensor(
                     out=s_tile[:, j * T:(j + 1) * T], in0=xnT[par][:, j * T:(j + 1) * T],
                     scalar=Dv[:, j:j + 1], in1=ps_cs[j % 2][:],
                     op0=ALU.mult, op1=ALU.add),
                 waits=[("se_pe", rec[("cmm", i, j)])], inc="se_dve")
            rec[("sstt", i, j)] = P.cnt["se_dve"]

        for j in range(32):
            if j < 8 and i >= 1:
                plan_sgT(i - 1, j)
            if j >= 8 and (j - 8) % 3 == 0 and (j - 8) // 3 < KD:
                plan_c((j - 8) // 3)
            bk0, bk1 = (j % 2) * 2, (j % 2) * 2 + 1
            g = i * 32 + j
            mmw = [(f"sm_sa{g % RS_A}", 16 * (g // RS_A + 1))]
            if j >= 2:
                mmw.append(("se_dve", rec[("hstt", i, j - 2)]))
            elif i >= 1:
                mmw.append(("se_dve", rec[("gdve", i - 1, 5 + 2 * j)]))
            slab = ringA[g % RS_A]
            for kc in range(KD):
                P.op("tensor",
                     lambda e, slab=slab, kc=kc, bk0=bk0, par=par: e.matmul(
                         out=ps_st[bk0][:], lhsT=slab[:, kc * 128:(kc + 1) * 128],
                         rhs=xnT[par][:, kc * T:(kc + 1) * T],
                         start=(kc == 0), stop=(kc == KD - 1)),
                     waits=mmw if kc == 0 else ())
            for kc in range(KD):
                P.op("tensor",
                     lambda e, slab=slab, kc=kc, bk1=bk1, par=par: e.matmul(
                         out=ps_st[bk1][:], lhsT=slab[:, (8 + kc) * 128:(9 + kc) * 128],
                         rhs=xnT[par][:, kc * T:(kc + 1) * T],
                         start=(kc == 0), stop=(kc == KD - 1)),
                     inc="se_pe" if kc == KD - 1 else None)
            rec[("Amm", i, j)] = P.cnt["se_pe"]
            plan_slabA_load(g + RS_A)

            aw = [("se_pe", rec[("Amm", i, j)])]
            if j >= 2:
                aw.append(("se_dve", rec[("ascr_rd", i, j - 2)]))
            elif i >= 1:
                aw.append(("se_dve", rec[("gdve", i - 1, 6 + j)]))
            P.op("scalar",
                 lambda e, j=j, bk0=bk0: e.activation(
                     out=act_scr[j % 2][:], in_=ps_st[bk0][:], func=AF.Sigmoid,
                     scale=w1s[:, j:j + 1]),
                 waits=aw, inc="se_act")
            rec[("sigA", i, j)] = P.cnt["se_act"]
            P.op("vector",
                 lambda e, j=j, bk0=bk0: e.tensor_mul(
                     out=dve_scr[j % 2][:], in0=act_scr[j % 2][:], in1=ps_st[bk0][:]),
                 waits=[("se_act", rec[("sigA", i, j)])], inc="se_dve")
            rec[("ascr_rd", i, j)] = P.cnt["se_dve"]
            hw = []
            if i >= 1 and j == 0:
                hw.append(("se_pe", rec[("w3mm", i - 1, 7)]))
            P.op("vector",
                 lambda e, j=j, bk1=bk1: e.scalar_tensor_tensor(
                     out=h_A[:, j * T:(j + 1) * T], in0=dve_scr[j % 2][:],
                     scalar=s12[:, j:j + 1], in1=ps_st[bk1][:],
                     op0=ALU.mult, op1=ALU.mult),
                 waits=hw, inc="se_dve")
            rec[("hstt", i, j)] = P.cnt["se_dve"]

        for b in range(3):
            prev_store = {0: 3, 1: 1, 2: 2}[b]
            w = [(f"sm_st{prev_store}", 16 * i)] if i >= 1 else []
            P.op("gpsimd",
                 lambda e, b=b, i=i: e.dma_start(
                     out=out_tok[:, b * D:(b + 1) * D],
                     in_=u_d[i * T + b * 128:i * T + (b + 1) * 128, :]),
                 waits=w, inc=f"sm_uo_{b}", inc_n=16)

        if i + 1 < NT:
            plan_fe_stats(i + 1)

        if debug and i == 0:
            P.op("sync", lambda e: e.dma_start(out=dbg["s0"][:], in_=s_tile[:]),
                 waits=[("se_dve", rec[("sstt", 0, 7)])], inc="sm_dbg", inc_n=16)
            P.op("sync", lambda e: e.dma_start(out=dbg["hA0"][:], in_=h_A[:]),
                 waits=[("se_dve", rec[("hstt", 0, 31)])], inc="sm_dbg", inc_n=16)

        def h_stage(name, src, dst, scale_e, scale_d, pos_base, mmrec, sttrec,
                    prev_sttrec, cb=None):
            for j in range(32):
                if cb is not None:
                    cb(j)
                bk = j % 4
                g = i * 72 + pos_base + j
                mmw = [(f"sm_sb{g % RS_B}", 16 * (g // RS_B + 1))]
                if j >= 4:
                    mmw.append(("se_dve", rec[(sttrec, i, j - 4)]))
                else:
                    if name == "h1":
                        mmw.append(("se_dve", rec[("ascr_rd", i, 30 + j // 2)]
                                    if j % 2 == 0 else rec[("hstt", i, 30 + j // 2)]))
                    else:
                        mmw.append(("se_dve", rec[("h1stt", i, 28 + j)]))
                slab = ringB[g % RS_B]
                for kc in range(KH):
                    kw = (mmw if kc == 0 else []) + (
                        [("se_dve", rec[(prev_sttrec, i, kc)])] if j == 0 else [])
                    P.op("tensor",
                         lambda e, slab=slab, kc=kc, bk=bk, src=src: e.matmul(
                             out=ps_st[bk][:], lhsT=slab[:, kc * 128:(kc + 1) * 128],
                             rhs=src[:, kc * T:(kc + 1) * T],
                             start=(kc == 0), stop=(kc == KH - 1)),
                         waits=kw,
                         inc="se_pe" if kc == KH - 1 else None)
                rec[(mmrec, i, j)] = P.cnt["se_pe"]
                plan_slabB_load(g + RS_B)

                aw = [("se_pe", rec[(mmrec, i, j)])]
                if j >= 2:
                    aw.append(("se_dve", rec[("ascr_rd2", name, i, j - 2)]))
                else:
                    aw.append(("se_dve", rec[("ascr_tail", name, i)]))
                P.op("scalar",
                     lambda e, j=j, bk=bk, scale_d=scale_d: e.activation(
                         out=act_scr[j % 2][:], in_=ps_st[bk][:], func=AF.Erf,
                         scale=scale_d[:, j:j + 1]),
                     waits=aw, inc="se_act")
                rec[("erf", name, i, j)] = P.cnt["se_act"]
                P.op("vector",
                     lambda e, j=j: e.tensor_scalar(
                         out=dve_scr[j % 2][:], in0=act_scr[j % 2][:], scalar1=0.5,
                         scalar2=0.5, op0=ALU.mult, op1=ALU.add),
                     waits=[("se_act", rec[("erf", name, i, j)])], inc="se_dve")
                rec[("ascr_rd2", name, i, j)] = P.cnt["se_dve"]
                P.op("vector",
                     lambda e, j=j, bk=bk, dst=dst, scale_e=scale_e: e.scalar_tensor_tensor(
                         out=dst[:, j * T:(j + 1) * T], in0=ps_st[bk][:],
                         scalar=scale_e[:, j:j + 1], in1=dve_scr[j % 2][:],
                         op0=ALU.mult, op1=ALU.mult),
                     inc="se_dve")
                rec[(sttrec, i, j)] = P.cnt["se_dve"]

        rec[("ascr_tail", "h1", i)] = rec[("ascr_rd", i, 31)]
        h_stage("h1", h_A, h_B, h1s, h1sd, 0, "h1mm", "h1stt", "hstt")
        rec[("ascr_tail", "h2", i)] = rec[("ascr_rd2", "h1", i, 31)]
        def h2_cb(j):
            if i + 1 < NT and j in (24, 26, 28, 30):
                plan_fe_block(i + 1, (j - 24) // 2)

        h_stage("h2", h_B, h_A, h2s, h2sd, 32, "h2mm", "h2stt", "h1stt",
                cb=h2_cb)
        rec[("ascr_tail", "w3", i)] = rec[("ascr_rd2", "h2", i, 31)]

        if debug and i == 0:
            P.op("sync", lambda e: e.dma_start(out=dbg["hB0"][:], in_=h_B[:]),
                 waits=[("se_dve", rec[("h1stt", 0, 31)])], inc="sm_dbg", inc_n=16)

    sg_order = [0, 2, 1, 3, 4, 6, 5, 7]

    def plan_sgT(it, pos):
        if True:
            i = it
            grp = sg_order[pos]
            b, hh = grp // 2, grp % 2
            bank = pos % 2
            tw = [("se_dve", rec[("sg", i, 4 * hh + 3)])]
            if pos >= 2:
                tw.append(("se_dve", rec[("final", i, sg_order[pos - 2])]))
            elif pos == 0 and i + 1 < NT:
                tw.append(("se_act", rec[("xnTcp", i + 1, 3)]))
            for q in range(4):
                j = hh * 4 + q
                P.op("tensor",
                     lambda e, j=j, b=b, bank=bank: e.transpose(
                         out=ps_cs[bank][:, (j % 4) * 128:((j % 4) + 1) * 128],
                         in_=s_tile[:, j * T + b * 128:j * T + (b + 1) * 128],
                         identity=idf[:]),
                     waits=tw if q == 0 else (),
                     inc="se_pe" if q == 3 else None)
            rec[("sgT", i, grp)] = P.cnt["se_pe"]
            sl = b if b < 3 else 0
            fw = [("se_pe", rec[("sgT", i, grp)]), (f"sm_uo_{b}", 16 * (i + 1))]
            P.op("vector",
                 lambda e, sl=sl, hh=hh, bank=bank: e.tensor_add(
                     out=out_tok[:, sl * D + hh * T:sl * D + (hh + 1) * T],
                     in0=out_tok[:, sl * D + hh * T:sl * D + (hh + 1) * T],
                     in1=ps_cs[bank][:]),
                 waits=fw, inc="se_dve")
            rec[("final", i, grp)] = P.cnt["se_dve"]
            rec[("final_cs", i, bank)] = rec[("final", i, grp)]
            if hh == 1:
                P.op("gpsimd",
                     lambda e, sl=sl, b=b, i=i: e.dma_start(
                         out=out_d[i * T + b * 128:i * T + (b + 1) * 128, :],
                         in_=out_tok[:, sl * D:(sl + 1) * D]),
                     waits=[("se_dve", rec[("final", i, grp)])],
                     inc=f"sm_st{b}", inc_n=16)
                if b == 0:
                    P.op("gpsimd",
                         lambda e, i=i: e.dma_start(
                             out=out_tok[:, 0:D],
                             in_=u_d[i * T + 3 * 128:i * T + 4 * 128, :]),
                         waits=[("sm_st0", 16 * (i + 1))],
                         inc="sm_uo_3", inc_n=16)

    def plan_backend_part2(i):
        for j in range(KD):
            bk = j % 4
            g = i * 72 + 64 + j
            mmw = [(f"sm_sb{g % RS_B}", 16 * (g // RS_B + 1))]
            if j >= 4:
                mmw.append(("se_dve", rec[("gdve", i, j - 4)]))
            else:
                mmw.append(("se_dve", rec[("h2stt", i, 28 + j)]))
            slab = ringB[g % RS_B]
            for kc in range(KH):
                kw = (mmw if kc == 0 else []) + (
                    [("se_dve", rec[("h2stt", i, kc)])] if j == 0 else [])
                P.op("tensor",
                     lambda e, slab=slab, kc=kc, bk=bk: e.matmul(
                         out=ps_st[bk][:], lhsT=slab[:, kc * 128:(kc + 1) * 128],
                         rhs=h_A[:, kc * T:(kc + 1) * T],
                         start=(kc == 0), stop=(kc == KH - 1)),
                     waits=kw,
                     inc="se_pe" if kc == KH - 1 else None)
            rec[("w3mm", i, j)] = P.cnt["se_pe"]
            plan_slabB_load(g + RS_B)

            aw = [("se_pe", rec[("w3mm", i, j)])]
            if j >= 2:
                aw.append(("se_dve", rec[("gdve", i, j - 2)]))
            else:
                aw.append(("se_dve", rec[("ascr_tail", "w3", i)]))
            P.op("scalar",
                 lambda e, j=j, bk=bk: e.activation(
                     out=act_scr[j % 2][:], in_=ps_st[bk][:], func=AF.Sigmoid,
                     scale=w3s[:, j:j + 1]),
                 waits=aw, inc="se_act")
            rec[("sigw", i, j)] = P.cnt["se_act"]
            P.op("vector",
                 lambda e, j=j, bk=bk: e.scalar_tensor_tensor(
                     out=g_scr[:], in0=ps_st[bk][:], scalar=w3s[:, j:j + 1],
                     in1=act_scr[j % 2][:], op0=ALU.mult, op1=ALU.mult),
                 waits=[("se_act", rec[("sigw", i, j)])], inc="se_dve")
            rec[("gdve", i, j)] = P.cnt["se_dve"]
            P.op("vector",
                 lambda e, j=j: e.tensor_mul(
                     out=s_tile[:, j * T:(j + 1) * T], in0=s_tile[:, j * T:(j + 1) * T],
                     in1=g_scr[:]),
                 inc="se_dve")
            rec[("sg", i, j)] = P.cnt["se_dve"]

        if debug and i == 0:
            P.op("sync", lambda e: e.dma_start(out=dbg["sg0"][:], in_=s_tile[:]),
                 waits=[("se_dve", rec[("sg", 0, 7)])], inc="sm_dbg", inc_n=16)

    plan_fe_stats(0)
    plan_prologue_compute()
    plan_fe_rest(0)
    for i in range(NT):
        plan_backend_part1(i)
        if i + 1 < NT:
            plan_fe_rest(i + 1)
        plan_backend_part2(i)
    for pos in range(8):
        plan_sgT(NT - 1, pos)

    final_waits = [(f"sm_st{b}", 16 * NT) for b in range(NB)]
    if debug:
        final_waits.append(("sm_dbg", P.cnt["sm_dbg"]))
    P.op("gpsimd", lambda e: e.engine_nop(), waits=final_waits)

    with nc.Block() as block:
        @block.sync
        def _(e):
            for f in P.prog["sync"]:
                f(e)

        @block.gpsimd
        def _(e):
            for f in P.prog["gpsimd"]:
                f(e)

        @block.tensor
        def _(e):
            for f in P.prog["tensor"]:
                f(e)

        @block.vector
        def _(e):
            for f in P.prog["vector"]:
                f(e)

        @block.scalar
        def _(e):
            for f in P.prog["scalar"]:
                f(e)

    return nc, sbuf_left


def _layout_inputs(inputs):
    f32 = np.float32
    u = np.ascontiguousarray(np.asarray(inputs["L0_attn_u"], dtype=f32))

    def slabs(w, nj, nkc):
        w = np.asarray(w, dtype=f32).reshape(nj, 128, nkc, 128)
        return w.transpose(0, 3, 2, 1)

    wA = np.ascontiguousarray(np.concatenate(
        [slabs(inputs["L0_ffn_w1_w"], 32, 8), slabs(inputs["L0_ffn_w2_w"], 32, 8)],
        axis=2))
    wH1 = np.ascontiguousarray(slabs(inputs["L0_ffn_h1_w"], 32, 32))
    wH2 = np.ascontiguousarray(slabs(inputs["L0_ffn_h2_w"], 32, 32))
    wW3 = np.ascontiguousarray(slabs(inputs["L0_ffn_w3_w"], 8, 32))
    BT = np.ascontiguousarray(np.asarray(inputs["L0_attn_B"], dtype=f32).T)
    CT = np.ascontiguousarray(np.asarray(inputs["L0_attn_C"], dtype=f32).T)

    def colmaj(v, nj):
        return np.ascontiguousarray(np.asarray(v, dtype=f32).reshape(nj, 128).T)

    common = {
        "wA": wA, "wH1": wH1, "wH2": wH2, "wW3": wW3,
        "BT": BT, "CT": CT,
        "w1s": colmaj(inputs["L0_ffn_w1_s"], 32),
        "w2s": colmaj(inputs["L0_ffn_w2_s"], 32),
        "h1s": colmaj(inputs["L0_ffn_h1_s"], 32),
        "h2s": colmaj(inputs["L0_ffn_h2_s"], 32),
        "w3s": colmaj(inputs["L0_ffn_w3_s"], 8),
        "Dv": colmaj(inputs["L0_attn_D"], 8),
        "Lam": np.asarray(inputs["L0_attn_Lambda"], dtype=f32).reshape(N, 1),
        "logdt": np.asarray(inputs["L0_attn_log_dt"], dtype=f32).reshape(1, 1),
        "idbf": np.eye(128, dtype=ml_dtypes.bfloat16),
        "idf": np.eye(128, dtype=f32),
    }
    in_maps = []
    for b in range(8):
        m = dict(common)
        m["u"] = np.ascontiguousarray(u[b])
        in_maps.append(m)
    return in_maps


def kernel(**inputs):
    in_maps = _layout_inputs(inputs)
    nc, _ = build_program(debug=DEBUG)
    res = run_bass_kernel_spmd(nc, in_maps, core_ids=list(range(8)))
    out = np.stack([r["out"] for r in res.results], axis=0)
    if DEBUG:
        kernel._dbg = res.results
    return out


# version 14
# speedup vs baseline: 1.1954x; 1.0024x over previous
import math
import numpy as np
import ml_dtypes
import concourse.bass as bass
import concourse.mybir as mybir
from concourse.bass_utils import run_bass_kernel_spmd

F32 = mybir.dt.float32
BF16 = mybir.dt.bfloat16
AF = mybir.ActivationFunctionType
ALU = mybir.AluOpType
AX = mybir.AxisListType

L, D, N, H = 8192, 1024, 64, 4096
T = 512
NT = L // T
NB = T // 128
KD = D // 128
KH = H // 128
RS_A = 4
RS_B = 4
EPS = 1e-5

DEBUG = False


class Plan:

    def __init__(self, nc):
        self.nc = nc
        self.prog = {k: [] for k in ("sync", "gpsimd", "tensor", "vector", "scalar")}
        self.cnt = {}
        self.waited = {}
        self.sems = {}

    def sem(self, name):
        if name not in self.sems:
            self.sems[name] = self.nc.alloc_semaphore(name)
            self.cnt[name] = 0
        return self.sems[name]

    def bump(self, name, k=1):
        self.cnt[name] += k
        return self.cnt[name]

    def op(self, eng, emit, waits=(), inc=None, inc_n=1):
        sems = self.sems
        real_waits = []
        for sname, val in waits:
            if val is None or val <= 0:
                continue
            key = (eng, sname)
            if self.waited.get(key, 0) >= val:
                continue
            self.waited[key] = val
            real_waits.append((sname, val))
        rec = None
        if inc is not None:
            rec = self.bump(inc, inc_n)

        def f(e, real_waits=real_waits, emit=emit, incname=inc, inc_amt=inc_n):
            for sname, val in real_waits:
                e.wait_ge(sems[sname], val)
            ins = emit(e)
            if incname is not None:
                ins.then_inc(sems[incname], inc_amt)

        self.prog[eng].append(f)
        return rec


def build_program(debug=False):
    nc = bass.Bass("TRN2", target_bir_lowering=False)

    u_d = nc.dram_tensor("u", [L, D], F32, kind="ExternalInput")
    wA_h = nc.dram_tensor("wA", [32, 128, 16, 128], F32, kind="ExternalInput")
    wH1_h = nc.dram_tensor("wH1", [32, 128, 32, 128], F32, kind="ExternalInput")
    wH2_h = nc.dram_tensor("wH2", [32, 128, 32, 128], F32, kind="ExternalInput")
    wW3_h = nc.dram_tensor("wW3", [8, 128, 32, 128], F32, kind="ExternalInput")
    BT_d = nc.dram_tensor("BT", [D, N], F32, kind="ExternalInput")
    CT_d = nc.dram_tensor("CT", [N, D], F32, kind="ExternalInput")
    w1s_d = nc.dram_tensor("w1s", [128, 32], F32, kind="ExternalInput")
    w2s_d = nc.dram_tensor("w2s", [128, 32], F32, kind="ExternalInput")
    h1s_d = nc.dram_tensor("h1s", [128, 32], F32, kind="ExternalInput")
    h2s_d = nc.dram_tensor("h2s", [128, 32], F32, kind="ExternalInput")
    w3s_d = nc.dram_tensor("w3s", [128, 8], F32, kind="ExternalInput")
    Dv_d = nc.dram_tensor("Dv", [128, 8], F32, kind="ExternalInput")
    lam_d = nc.dram_tensor("Lam", [N, 1], F32, kind="ExternalInput")
    ld_d = nc.dram_tensor("logdt", [1, 1], F32, kind="ExternalInput")
    idbf_d = nc.dram_tensor("idbf", [128, 128], BF16, kind="ExternalInput")
    idf_d = nc.dram_tensor("idf", [128, 128], F32, kind="ExternalInput")
    out_d = nc.dram_tensor("out", [L, D], F32, kind="ExternalOutput")

    scrA = nc.dram_tensor("scrA", [32, 128, 16, 128], BF16, kind="Internal")
    scrH1 = nc.dram_tensor("scrH1", [32, 128, 32, 128], BF16, kind="Internal")
    scrH2 = nc.dram_tensor("scrH2", [32, 128, 32, 128], BF16, kind="Internal")
    scrW3 = nc.dram_tensor("scrW3", [8, 128, 32, 128], BF16, kind="Internal")

    dbg = {}
    if debug:
        dbg["xnT0"] = nc.dram_tensor("dbg_xnT0", [128, 4096], BF16, kind="ExternalOutput")
        dbg["xs0"] = nc.dram_tensor("dbg_xs0", [N, T], F32, kind="ExternalOutput")
        dbg["s0"] = nc.dram_tensor("dbg_s0", [128, 4096], F32, kind="ExternalOutput")
        dbg["hA0"] = nc.dram_tensor("dbg_hA0", [128, 16384], BF16, kind="ExternalOutput")
        dbg["hB0"] = nc.dram_tensor("dbg_hB0", [128, 16384], BF16, kind="ExternalOutput")
        dbg["sg0"] = nc.dram_tensor("dbg_sg0", [128, 4096], F32, kind="ExternalOutput")
        dbg["stats0"] = nc.dram_tensor("dbg_stats0", [128, 16], F32, kind="ExternalOutput")
        dbg["u0"] = nc.dram_tensor("dbg_u0", [128, 1024], F32, kind="ExternalOutput")

    A = nc.alloc_sbuf_tensor
    u_tok = A("u_tok", [128, NB * D], F32)
    xnT = [A(f"xnT{p}", [128, KD * T], BF16) for p in range(2)]
    xn_blk = [A(f"xn_blk{p}", [128, D], BF16) for p in range(3)]
    h_A = A("h_A", [128, KH * T], BF16)
    h_B = A("h_B", [128, KH * T], BF16)
    s_tile = A("s_tile", [128, KD * T], F32)
    out_tok = A("out_tok", [128, 3 * D], F32)
    ringA = [A(f"ringA{r}", [128, 16 * 128], BF16) for r in range(RS_A)]
    ringB = [A(f"ringB{r}", [128, 32 * 128], BF16) for r in range(RS_B)]
    act_scr = [A(f"act_scr{p}", [128, T], F32) for p in range(2)]
    dve_scr = [A(f"dve_scr{p}", [128, T], F32) for p in range(2)]
    g_scr = A("g_scr", [128, T], F32)
    A_rep = A("A_rep", [N, T], F32)
    xs = A("xs", [N, T], F32)
    xs_bf = A("xs_bf", [N, T], BF16)
    carry = A("carry", [N, 1], F32)
    a_sb = A("a_sb", [N, 1], F32)
    lam_sb = A("lam_sb", [N, 1], F32)
    ld_sb = A("ld_sb", [1, 1], F32)
    lamdt = A("lamdt", [N, 1], F32)
    e1 = A("e1", [N, 1], F32)
    ones1 = A("ones1", [1, N], F32)
    BT_sb = A("BT_sb", [128, KD * N], BF16)
    CT_sb = A("CT_sb", [N, D], BF16)
    w1s = A("w1s_sb", [128, 32], F32)
    w2s = A("w2s_sb", [128, 32], F32)
    h1s = A("h1s_sb", [128, 32], F32)
    h2s = A("h2s_sb", [128, 32], F32)
    w3s = A("w3s_sb", [128, 8], F32)
    Dv = A("Dv_sb", [128, 8], F32)
    s12 = A("s12_sb", [128, 32], F32)
    h1sd = A("h1sd_sb", [128, 32], F32)
    h2sd = A("h2sd_sb", [128, 32], F32)
    idbf = A("idbf_sb", [128, 128], BF16)
    idf = A("idf_sb", [128, 128], F32)
    sums = A("sums", [128, NB], F32)
    sumsq = A("sumsq", [128, NB], F32)
    mcol = A("mcol", [128, NB], F32)
    vpe = A("vpe", [128, NB], F32)
    rstd = A("rstd", [128, NB], F32)
    nmr = A("nmr", [128, NB], F32)
    nt0 = A("nt0", [128, NB], F32)
    nt1 = A("nt1", [128, NB], F32)

    ps_st = [nc.alloc_psum_tensor(f"ps_st{k}", [128, T], F32) for k in range(4)]
    ps_bu = nc.alloc_psum_tensor("ps_bu", [N, T], F32)
    ps_xnT = nc.alloc_psum_tensor("ps_xnT", [128, 1024], BF16)
    ps_cs = [nc.alloc_psum_tensor(f"ps_cs{k}", [128, T], F32) for k in range(2)]

    sbuf_left = nc.sbuf_bytes_remaining

    P = Plan(nc)
    for s in (["sm_castW3", "sm_const", "se_pe", "se_act", "se_dve", "sm_dbg"]
              + [f"sm_cA{c}" for c in range(4)]
              + [f"sm_cH1{c}" for c in range(4)]
              + [f"sm_cH2{c}" for c in range(4)]
              + [f"sm_u_{b}" for b in range(NB)]
              + [f"sm_uo_{b}" for b in range(NB)]
              + [f"sm_sa{r}" for r in range(RS_A)]
              + [f"sm_sb{r}" for r in range(RS_B)]
              + [f"sm_st{b}" for b in range(NB)]):
        P.sem(s)

    rec = {}

    def plan_slabA_load(g):
        if g >= NT * 32:
            return
        i, j = g // 32, g % 32
        waits = []
        if i == 0:
            waits.append((f"sm_cA{j // 8}", 16))
        if g >= RS_A:
            gp = g - RS_A
            waits.append(("se_pe", rec[("Amm", gp // 32, gp % 32)]))
        P.op("sync",
             lambda e, j=j, g=g: e.dma_start(
                 out=ringA[g % RS_A][:],
                 in_=scrA[j].rearrange("p s c -> p (s c)")),
             waits=waits, inc=f"sm_sa{g % RS_A}", inc_n=16)

    def big_rec(g):
        i, pos = g // 72, g % 72
        if pos < 32:
            return rec[("h1mm", i, pos)]
        if pos < 64:
            return rec[("h2mm", i, pos - 32)]
        return rec[("w3mm", i, pos - 64)]

    def plan_slabB_load(g):
        if g >= NT * 72:
            return
        i, pos = g // 72, g % 72
        if pos < 32:
            scr, j, cast, castv = scrH1, pos, f"sm_cH1{pos // 8}", 8 * 16
        elif pos < 64:
            scr, j, cast, castv = scrH2, pos - 32, f"sm_cH2{(pos - 32) // 8}", 8 * 16
        else:
            scr, j, cast, castv = scrW3, pos - 64, "sm_castW3", 8 * 16
        waits = []
        if i == 0:
            waits.append((cast, 16))
        if g >= RS_B:
            waits.append(("se_pe", big_rec(g - RS_B)))
        P.op("sync",
             lambda e, scr=scr, j=j, g=g: e.dma_start(
                 out=ringB[g % RS_B][:],
                 in_=scr[j].rearrange("p kc c -> p (kc c)")),
             waits=waits, inc=f"sm_sb{g % RS_B}", inc_n=16)

    for b in range(NB):
        P.op("sync",
             lambda e, b=b: e.dma_start(
                 out=u_tok[:, b * D:(b + 1) * D],
                 in_=u_d[b * 128:(b + 1) * 128, :]),
             inc=f"sm_u_{b}", inc_n=16)

    def cast_chunk(dst, src, c, nj, sem):
        P.op("gpsimd",
             lambda e, dst=dst, src=src, c=c, nj=nj: e.dma_start(
                 out=dst[c * nj:(c + 1) * nj].rearrange("j p s c -> (j p) (s c)"),
                 in_=src[c * nj:(c + 1) * nj].rearrange("j p s c -> (j p) (s c)")),
             inc=sem, inc_n=16)

    for c in range(4):
        cast_chunk(scrA, wA_h, c, 8, f"sm_cA{c}")
    for c in range(4):
        cast_chunk(scrH1, wH1_h, c, 8, f"sm_cH1{c}")
    for c in range(4):
        cast_chunk(scrH2, wH2_h, c, 8, f"sm_cH2{c}")
    cast_chunk(scrW3, wW3_h, 0, 8, "sm_castW3")

    consts = [
        (lambda: lam_sb[:], lambda: lam_d[:]),
        (lambda: ld_sb[:], lambda: ld_d[:]),
        (lambda: w1s[:], lambda: w1s_d[:]),
        (lambda: w2s[:], lambda: w2s_d[:]),
        (lambda: h1s[:], lambda: h1s_d[:]),
        (lambda: h2s[:], lambda: h2s_d[:]),
        (lambda: w3s[:], lambda: w3s_d[:]),
        (lambda: Dv[:], lambda: Dv_d[:]),
        (lambda: idbf[:], lambda: idbf_d[:]),
        (lambda: idf[:], lambda: idf_d[:]),
        (lambda: g_scr[:, 0:KD * N].rearrange("p (k n) -> p k n", n=N),
         lambda: BT_d[:].rearrange("(k p) n -> p k n", p=128)),
    ]
    for dstf, srcf in consts:
        P.op("sync", lambda e, dstf=dstf, srcf=srcf: e.dma_start(out=dstf(), in_=srcf()),
             inc="sm_const", inc_n=16)
    consts_done = P.cnt["sm_const"]

    def plan_prologue_compute():
        plan_prologue_compute_body()

    def plan_prologue_compute_body():
        P.op("vector", lambda e: e.memset(ones1[:], 1.0), inc="se_dve")
        rec["ones"] = P.cnt["se_dve"]
        P.op("tensor",
             lambda e: e.matmul(out=ps_bu[0:N, 0:1], lhsT=ones1[0:1, :],
                                rhs=ld_sb[0:1, 0:1], start=True, stop=True),
             waits=[("sm_const", consts_done), ("se_dve", rec["ones"])],
             inc="se_pe")
        rec["dtb"] = P.cnt["se_pe"]
        P.op("vector",
             lambda e: e.tensor_add(out=lamdt[:], in0=lam_sb[:], in1=ps_bu[0:N, 0:1]),
             waits=[("se_pe", rec["dtb"])], inc="se_dve")
        rec["lamdt"] = P.cnt["se_dve"]
        P.op("scalar",
             lambda e: e.activation(out=e1[:], in_=lamdt[:], func=AF.Exp),
             waits=[("se_dve", rec["lamdt"])], inc="se_act")
        rec["e1"] = P.cnt["se_act"]
        P.op("scalar",
             lambda e: e.activation(out=a_sb[:], in_=e1[:], func=AF.Exp, scale=-1.0),
             waits=[("se_act", rec["e1"])], inc="se_act")
        rec["a"] = P.cnt["se_act"]
        pv = P.op("vector", lambda e: e.memset(A_rep[:], 1.0), inc="se_dve")
        pv = P.op("vector",
                  lambda e: e.tensor_scalar(out=A_rep[:], in0=A_rep[:], scalar1=a_sb[:, 0:1],
                                            scalar2=None, op0=ALU.mult),
                  waits=[("se_act", rec["a"]), ("se_dve", pv)], inc="se_dve")
        pv = P.op("vector", lambda e: e.memset(carry[:], 0.0),
                  waits=[("se_dve", pv)], inc="se_dve")
        pv = P.op("vector", lambda e: e.tensor_mul(out=s12[:], in0=w1s[:], in1=w2s[:]),
                  waits=[("sm_const", consts_done), ("se_dve", pv)], inc="se_dve")
        isq2 = 1.0 / math.sqrt(2.0)
        pv = P.op("vector", lambda e: e.tensor_scalar(out=h1sd[:], in0=h1s[:], scalar1=isq2,
                                                      scalar2=None, op0=ALU.mult),
                  waits=[("se_dve", pv)], inc="se_dve")
        pv = P.op("vector", lambda e: e.tensor_scalar(out=h2sd[:], in0=h2s[:], scalar1=isq2,
                                                      scalar2=None, op0=ALU.mult),
                  waits=[("se_dve", pv)], inc="se_dve")
        pv = P.op("vector", lambda e: e.tensor_copy(out=BT_sb[:], in_=g_scr[:, 0:KD * N]),
                  waits=[("se_dve", pv)], inc="se_dve")
        for half in range(2):
            P.op("sync",
                 lambda e, half=half: e.dma_start(
                     out=g_scr[0:N, 0:T], in_=CT_d[:, half * T:(half + 1) * T]),
                 waits=[("se_dve", pv)], inc="sm_const", inc_n=16)
            ld = P.cnt["sm_const"]
            pv = P.op("vector",
                      lambda e, half=half: e.tensor_copy(
                          out=CT_sb[:, half * T:(half + 1) * T], in_=g_scr[0:N, 0:T]),
                      waits=[("sm_const", ld), ("se_dve", pv)], inc="se_dve")
        rec["consts"] = P.cnt["se_dve"]

        for g in range(RS_A):
            plan_slabA_load(g)
        for g in range(RS_B):
            plan_slabB_load(g)


    def u_sem(i, b):
        return (f"sm_u_{b}", 16 * (i + 1))

    def plan_fe_stats(i):
        par = i % 2
        pv = None
        sq_prev = [None, None]
        for b in range(NB):
            uw = u_sem(i, b)
            P.op("vector",
                 lambda e, b=b, par=par: e.reduce_sum(
                     out=sums[:, b:b + 1], in_=u_tok[:, b * D:(b + 1) * D],
                     axis=AX.X),
                 waits=[uw], inc="se_dve")
            sqw = [uw]
            if sq_prev[b % 2] is not None:
                sqw.append(("se_dve", sq_prev[b % 2]))
            pv = P.op("vector",
                      lambda e, b=b, par=par: e.scalar_tensor_tensor(
                          out=xnT[par][:].bitcast(F32)[:, (b % 2) * D:(b % 2 + 1) * D],
                          in0=u_tok[:, b * D:(b + 1) * D],
                          scalar=1.0, in1=u_tok[:, b * D:(b + 1) * D],
                          op0=ALU.bypass, op1=ALU.mult, accum_out=sumsq[:, b:b + 1]),
                      waits=sqw, inc="se_dve")
            sq_prev[b % 2] = pv
        def chain(emitf):
            nonlocal pv
            pv = P.op("vector", emitf, waits=[("se_dve", pv)], inc="se_dve")

        chain(lambda e: e.tensor_scalar(
            out=mcol[:], in0=sums[:], scalar1=1.0 / D, scalar2=None, op0=ALU.mult))
        chain(lambda e: e.tensor_scalar(
            out=vpe[:], in0=sumsq[:], scalar1=1.0 / D, scalar2=EPS,
            op0=ALU.mult, op1=ALU.add))
        chain(lambda e: e.tensor_mul(out=nt0[:], in0=mcol[:], in1=mcol[:]))
        chain(lambda e: e.tensor_sub(out=vpe[:], in0=vpe[:], in1=nt0[:]))
        chain(lambda e: e.tensor_scalar(
            out=rstd[:], in0=vpe[:], scalar1=-0.5, scalar2=1.5,
            op0=ALU.mult, op1=ALU.add))
        for _ in range(4):
            chain(lambda e: e.tensor_mul(out=nt0[:], in0=rstd[:], in1=rstd[:]))
            chain(lambda e: e.tensor_mul(out=nt1[:], in0=nt0[:], in1=vpe[:]))
            chain(lambda e: e.tensor_scalar(
                out=nt0[:], in0=nt1[:], scalar1=-0.5, scalar2=1.5,
                op0=ALU.mult, op1=ALU.add))
            chain(lambda e: e.tensor_mul(out=rstd[:], in0=rstd[:], in1=nt0[:]))
        rec[("stats", i)] = P.cnt["se_dve"]
        for b in range(3):
            plan_xn(i, b)

    def plan_xn(i, b):
        par = i % 2
        waits = [u_sem(i, b), ("se_dve", rec[("stats", i)])]
        if b == 3:
            waits.append(("se_pe", rec[("xnTps", i, 0)]))
        elif i >= 1:
            waits.append(("se_pe", rec[("xnTps", i - 1, 3 if b == 0 else b)]))
        P.op("vector",
             lambda e, b=b, par=par: e.tensor_scalar(
                 out=xn_blk[b % 3][:], in0=u_tok[:, b * D:(b + 1) * D],
                 scalar1=mcol[:, b:b + 1], scalar2=rstd[:, b:b + 1],
                 op0=ALU.subtract, op1=ALU.mult),
             waits=waits, inc="se_dve")
        rec[("xn", i, b)] = P.cnt["se_dve"]

    def plan_fe_block(i, b):
        par = i % 2
        if True:
            tb = (lambda: ps_xnT[:]) if b % 2 == 0 else (lambda: ps_cs[0][:].bitcast(BF16))
            tw = [("se_dve", rec[("xn", i, b)])]
            if (i, b) not in ((0, 0), (0, 1)):
                pb = (i, b - 2) if b >= 2 else (i - 1, b + 2)
                tw.append(("se_act", rec[("xnTcp", pb[0], pb[1])]))
            for k in range(KD):
                P.op("tensor",
                     lambda e, b=b, k=k, tb=tb: e.transpose(
                         out=tb()[:, k * 128:(k + 1) * 128],
                         in_=xn_blk[b % 3][:, k * 128:(k + 1) * 128],
                         identity=idbf[:]),
                     waits=tw if k == 0 else (),
                     inc="se_pe" if k == KD - 1 else None)
            rec[("xnTps", i, b)] = P.cnt["se_pe"]
            if b == 0:
                plan_xn(i, 3)
            cw = [("se_pe", rec[("xnTps", i, b)])]
            if b == 0 and i >= 2:
                cw.append(("se_pe", rec[("Amm", i - 2, 31)]))
            P.op("scalar",
                 lambda e, b=b, par=par, tb=tb: e.copy(
                     out=xnT[par][:].rearrange("p (k t) -> p k t", t=T)[:, :, b * 128:(b + 1) * 128],
                     in_=tb().rearrange("p (k t) -> p k t", t=128)),
                 waits=cw, inc="se_act")
            rec[("xnTcp", i, b)] = P.cnt["se_act"]

    def plan_fe_rest(i):
        par = i % 2
        if i == 0:
            for b in range(NB):
                plan_fe_block(0, b)
        buw = [("se_act", rec[("xnTcp", i, 3)]), ("se_dve", rec["consts"])]
        if i >= 1:
            buw.append(("se_dve", rec[("scan", i - 1)]))
        for k in range(KD):
            P.op("tensor",
                 lambda e, k=k, par=par: e.matmul(
                     out=ps_bu[:], lhsT=BT_sb[:, k * N:(k + 1) * N],
                     rhs=xnT[par][:, k * T:(k + 1) * T],
                     start=(k == 0), stop=(k == KD - 1)),
                 waits=buw if k == 0 else (),
                 inc="se_pe" if k == KD - 1 else None)
        rec[("bu", i)] = P.cnt["se_pe"]

        sv = P.op("vector",
                  lambda e: e.tensor_tensor_scan(
                      out=xs[:], data0=A_rep[:], data1=ps_bu[:],
                      initial=carry[:, 0:1], op0=ALU.mult, op1=ALU.add),
                  waits=[("se_pe", rec[("bu", i)])], inc="se_dve")
        sv = P.op("vector", lambda e: e.tensor_copy(out=carry[:, 0:1], in_=xs[:, T - 1:T]),
                  waits=[("se_dve", sv)], inc="se_dve")
        P.op("vector", lambda e: e.tensor_copy(out=xs_bf[:], in_=xs[:]),
             waits=[("se_dve", sv)], inc="se_dve")
        rec[("scan", i)] = P.cnt["se_dve"]

        if debug and i == 0:
            P.op("vector", lambda e: e.tensor_copy(out=nt1[:], in_=rstd[:]),
                 inc="se_dve")
            rec["dbg_rstd_cp"] = P.cnt["se_dve"]
            P.op("sync", lambda e: e.dma_start(out=dbg["stats0"][:, 0:4], in_=sums[:]),
                 waits=[("se_dve", rec["dbg_rstd_cp"])], inc="sm_dbg", inc_n=16)
            P.op("sync", lambda e: e.dma_start(out=dbg["stats0"][:, 4:8], in_=sumsq[:]),
                 inc="sm_dbg", inc_n=16)
            P.op("sync", lambda e: e.dma_start(out=dbg["stats0"][:, 8:12], in_=nt1[:]),
                 inc="sm_dbg", inc_n=16)
            P.op("sync", lambda e: e.dma_start(out=dbg["stats0"][:, 12:16], in_=mcol[:]),
                 inc="sm_dbg", inc_n=16)
            P.op("sync", lambda e: e.dma_start(out=dbg["u0"][:], in_=u_tok[:, 0:D]),
                 inc="sm_dbg", inc_n=16)
            P.op("sync", lambda e: e.dma_start(out=dbg["xnT0"][:], in_=xnT[0][:]),
                 waits=[("se_act", rec[("xnTcp", 0, 3)])], inc="sm_dbg", inc_n=16)
            P.op("sync", lambda e: e.dma_start(out=dbg["xs0"][:], in_=xs[:]),
                 waits=[("se_dve", rec[("scan", 0)])], inc="sm_dbg", inc_n=16)

    def plan_backend_part1(i):
        par = i % 2
        if i + 1 < NT:
            for b in range(NB):
                w = [("se_dve", rec[("xn", i, 3)])] if b == 0 else []
                P.op("gpsimd",
                     lambda e, b=b, i=i: e.dma_start(
                         out=u_tok[:, b * D:(b + 1) * D],
                         in_=u_d[(i + 1) * T + b * 128:(i + 1) * T + (b + 1) * 128, :]),
                     waits=w, inc=f"sm_u_{b}", inc_n=16)

        def plan_c(j):
            w = [("se_dve", rec[("scan", i)])]
            if j >= 2:
                w.append(("se_dve", rec[("sstt", i, j - 2)]))
            elif i >= 1:
                w.append(("se_dve", rec[("final_cs", i - 1, j % 2)]))
            P.op("tensor",
                 lambda e, j=j: e.matmul(
                     out=ps_cs[j % 2][:], lhsT=CT_sb[:, j * 128:(j + 1) * 128],
                     rhs=xs_bf[:], start=True, stop=True),
                 waits=w, inc="se_pe")
            rec[("cmm", i, j)] = P.cnt["se_pe"]
            P.op("vector",
                 lambda e, j=j, par=par: e.scalar_tensor_tensor(
                     out=s_tile[:, j * T:(j + 1) * T], in0=xnT[par][:, j * T:(j + 1) * T],
                     scalar=Dv[:, j:j + 1], in1=ps_cs[j % 2][:],
                     op0=ALU.mult, op1=ALU.add),
                 waits=[("se_pe", rec[("cmm", i, j)])], inc="se_dve")
            rec[("sstt", i, j)] = P.cnt["se_dve"]

        for j in range(32):
            if j < 8 and i >= 1:
                plan_sgT(i - 1, j)
            if j >= 8 and (j - 8) % 3 == 0 and (j - 8) // 3 < KD:
                plan_c((j - 8) // 3)
            bk0, bk1 = (j % 2) * 2, (j % 2) * 2 + 1
            g = i * 32 + j
            mmw = [(f"sm_sa{g % RS_A}", 16 * (g // RS_A + 1))]
            if j >= 2:
                mmw.append(("se_dve", rec[("hstt", i, j - 2)]))
            elif i >= 1:
                mmw.append(("se_dve", rec[("gdve", i - 1, 5 + 2 * j)]))
            slab = ringA[g % RS_A]
            for kc in range(KD):
                P.op("tensor",
                     lambda e, slab=slab, kc=kc, bk0=bk0, par=par: e.matmul(
                         out=ps_st[bk0][:], lhsT=slab[:, kc * 128:(kc + 1) * 128],
                         rhs=xnT[par][:, kc * T:(kc + 1) * T],
                         start=(kc == 0), stop=(kc == KD - 1)),
                     waits=mmw if kc == 0 else ())
            for kc in range(KD):
                P.op("tensor",
                     lambda e, slab=slab, kc=kc, bk1=bk1, par=par: e.matmul(
                         out=ps_st[bk1][:], lhsT=slab[:, (8 + kc) * 128:(9 + kc) * 128],
                         rhs=xnT[par][:, kc * T:(kc + 1) * T],
                         start=(kc == 0), stop=(kc == KD - 1)),
                     inc="se_pe" if kc == KD - 1 else None)
            rec[("Amm", i, j)] = P.cnt["se_pe"]
            plan_slabA_load(g + RS_A)

            aw = [("se_pe", rec[("Amm", i, j)])]
            if j >= 2:
                aw.append(("se_dve", rec[("ascr_rd", i, j - 2)]))
            elif i >= 1:
                aw.append(("se_dve", rec[("gdve", i - 1, 6 + j)]))
            P.op("scalar",
                 lambda e, j=j, bk0=bk0: e.activation(
                     out=act_scr[j % 2][:], in_=ps_st[bk0][:], func=AF.Sigmoid,
                     scale=w1s[:, j:j + 1]),
                 waits=aw, inc="se_act")
            rec[("sigA", i, j)] = P.cnt["se_act"]
            P.op("vector",
                 lambda e, j=j, bk0=bk0: e.tensor_mul(
                     out=dve_scr[j % 2][:], in0=act_scr[j % 2][:], in1=ps_st[bk0][:]),
                 waits=[("se_act", rec[("sigA", i, j)])], inc="se_dve")
            rec[("ascr_rd", i, j)] = P.cnt["se_dve"]
            hw = []
            if i >= 1 and j == 0:
                hw.append(("se_pe", rec[("w3mm", i - 1, 7)]))
            P.op("vector",
                 lambda e, j=j, bk1=bk1: e.scalar_tensor_tensor(
                     out=h_A[:, j * T:(j + 1) * T], in0=dve_scr[j % 2][:],
                     scalar=s12[:, j:j + 1], in1=ps_st[bk1][:],
                     op0=ALU.mult, op1=ALU.mult),
                 waits=hw, inc="se_dve")
            rec[("hstt", i, j)] = P.cnt["se_dve"]

        for b in range(3):
            prev_store = {0: 3, 1: 1, 2: 2}[b]
            w = [(f"sm_st{prev_store}", 16 * i)] if i >= 1 else []
            P.op("gpsimd",
                 lambda e, b=b, i=i: e.dma_start(
                     out=out_tok[:, b * D:(b + 1) * D],
                     in_=u_d[i * T + b * 128:i * T + (b + 1) * 128, :]),
                 waits=w, inc=f"sm_uo_{b}", inc_n=16)

        if i + 1 < NT:
            plan_fe_stats(i + 1)

        if debug and i == 0:
            P.op("sync", lambda e: e.dma_start(out=dbg["s0"][:], in_=s_tile[:]),
                 waits=[("se_dve", rec[("sstt", 0, 7)])], inc="sm_dbg", inc_n=16)
            P.op("sync", lambda e: e.dma_start(out=dbg["hA0"][:], in_=h_A[:]),
                 waits=[("se_dve", rec[("hstt", 0, 31)])], inc="sm_dbg", inc_n=16)

        def h_stage(name, src, dst, scale_e, scale_d, pos_base, mmrec, sttrec,
                    prev_sttrec, cb=None):
            for j in range(32):
                if cb is not None:
                    cb(j)
                bk = j % 4
                g = i * 72 + pos_base + j
                mmw = [(f"sm_sb{g % RS_B}", 16 * (g // RS_B + 1))]
                if j >= 4:
                    mmw.append(("se_dve", rec[(sttrec, i, j - 4)]))
                else:
                    if name == "h1":
                        mmw.append(("se_dve", rec[("ascr_rd", i, 30 + j // 2)]
                                    if j % 2 == 0 else rec[("hstt", i, 30 + j // 2)]))
                    else:
                        mmw.append(("se_dve", rec[("h1stt", i, 28 + j)]))
                slab = ringB[g % RS_B]
                for kc in range(KH):
                    kw = (mmw if kc == 0 else []) + (
                        [("se_dve", rec[(prev_sttrec, i, kc)])] if j == 0 else [])
                    P.op("tensor",
                         lambda e, slab=slab, kc=kc, bk=bk, src=src: e.matmul(
                             out=ps_st[bk][:], lhsT=slab[:, kc * 128:(kc + 1) * 128],
                             rhs=src[:, kc * T:(kc + 1) * T],
                             start=(kc == 0), stop=(kc == KH - 1)),
                         waits=kw,
                         inc="se_pe" if kc == KH - 1 else None)
                rec[(mmrec, i, j)] = P.cnt["se_pe"]
                plan_slabB_load(g + RS_B)

                aw = [("se_pe", rec[(mmrec, i, j)])]
                if j >= 2:
                    aw.append(("se_dve", rec[("ascr_rd2", name, i, j - 2)]))
                else:
                    aw.append(("se_dve", rec[("ascr_tail", name, i)]))
                P.op("scalar",
                     lambda e, j=j, bk=bk, scale_d=scale_d: e.activation(
                         out=act_scr[j % 2][:], in_=ps_st[bk][:], func=AF.Erf,
                         scale=scale_d[:, j:j + 1]),
                     waits=aw, inc="se_act")
                rec[("erf", name, i, j)] = P.cnt["se_act"]
                P.op("vector",
                     lambda e, j=j: e.tensor_scalar(
                         out=dve_scr[j % 2][:], in0=act_scr[j % 2][:], scalar1=0.5,
                         scalar2=0.5, op0=ALU.mult, op1=ALU.add),
                     waits=[("se_act", rec[("erf", name, i, j)])], inc="se_dve")
                rec[("ascr_rd2", name, i, j)] = P.cnt["se_dve"]
                P.op("vector",
                     lambda e, j=j, bk=bk, dst=dst, scale_e=scale_e: e.scalar_tensor_tensor(
                         out=dst[:, j * T:(j + 1) * T], in0=ps_st[bk][:],
                         scalar=scale_e[:, j:j + 1], in1=dve_scr[j % 2][:],
                         op0=ALU.mult, op1=ALU.mult),
                     inc="se_dve")
                rec[(sttrec, i, j)] = P.cnt["se_dve"]

        rec[("ascr_tail", "h1", i)] = rec[("ascr_rd", i, 31)]
        h_stage("h1", h_A, h_B, h1s, h1sd, 0, "h1mm", "h1stt", "hstt")
        rec[("ascr_tail", "h2", i)] = rec[("ascr_rd2", "h1", i, 31)]
        def h2_cb(j):
            if i + 1 < NT and j in (24, 26, 28, 30):
                plan_fe_block(i + 1, (j - 24) // 2)

        h_stage("h2", h_B, h_A, h2s, h2sd, 32, "h2mm", "h2stt", "h1stt",
                cb=h2_cb)
        rec[("ascr_tail", "w3", i)] = rec[("ascr_rd2", "h2", i, 31)]

        if debug and i == 0:
            P.op("sync", lambda e: e.dma_start(out=dbg["hB0"][:], in_=h_B[:]),
                 waits=[("se_dve", rec[("h1stt", 0, 31)])], inc="sm_dbg", inc_n=16)

    sg_order = [0, 2, 1, 3, 4, 6, 5, 7]

    def plan_sgT(it, pos):
        if True:
            i = it
            grp = sg_order[pos]
            b, hh = grp // 2, grp % 2
            bank = pos % 2
            tw = [("se_dve", rec[("sg", i, 4 * hh + 3)])]
            if pos >= 2:
                tw.append(("se_dve", rec[("final", i, sg_order[pos - 2])]))
            elif pos == 0 and i + 1 < NT:
                tw.append(("se_act", rec[("xnTcp", i + 1, 3)]))
            for q in range(4):
                j = hh * 4 + q
                P.op("tensor",
                     lambda e, j=j, b=b, bank=bank: e.transpose(
                         out=ps_cs[bank][:, (j % 4) * 128:((j % 4) + 1) * 128],
                         in_=s_tile[:, j * T + b * 128:j * T + (b + 1) * 128],
                         identity=idf[:]),
                     waits=tw if q == 0 else (),
                     inc="se_pe" if q == 3 else None)
            rec[("sgT", i, grp)] = P.cnt["se_pe"]
            sl = b if b < 3 else 0
            fw = [("se_pe", rec[("sgT", i, grp)]), (f"sm_uo_{b}", 16 * (i + 1))]
            P.op("vector",
                 lambda e, sl=sl, hh=hh, bank=bank: e.tensor_add(
                     out=out_tok[:, sl * D + hh * T:sl * D + (hh + 1) * T],
                     in0=out_tok[:, sl * D + hh * T:sl * D + (hh + 1) * T],
                     in1=ps_cs[bank][:]),
                 waits=fw, inc="se_dve")
            rec[("final", i, grp)] = P.cnt["se_dve"]
            rec[("final_cs", i, bank)] = rec[("final", i, grp)]
            if hh == 1:
                P.op("gpsimd",
                     lambda e, sl=sl, b=b, i=i: e.dma_start(
                         out=out_d[i * T + b * 128:i * T + (b + 1) * 128, :],
                         in_=out_tok[:, sl * D:(sl + 1) * D]),
                     waits=[("se_dve", rec[("final", i, grp)])],
                     inc=f"sm_st{b}", inc_n=16)
                if b == 0:
                    P.op("gpsimd",
                         lambda e, i=i: e.dma_start(
                             out=out_tok[:, 0:D],
                             in_=u_d[i * T + 3 * 128:i * T + 4 * 128, :]),
                         waits=[("sm_st0", 16 * (i + 1))],
                         inc="sm_uo_3", inc_n=16)

    def plan_backend_part2(i):
        for j in range(KD):
            bk = j % 4
            g = i * 72 + 64 + j
            mmw = [(f"sm_sb{g % RS_B}", 16 * (g // RS_B + 1))]
            if j >= 4:
                mmw.append(("se_dve", rec[("gdve", i, j - 4)]))
            else:
                mmw.append(("se_dve", rec[("h2stt", i, 28 + j)]))
            slab = ringB[g % RS_B]
            for kc in range(KH):
                kw = (mmw if kc == 0 else []) + (
                    [("se_dve", rec[("h2stt", i, kc)])] if j == 0 else [])
                P.op("tensor",
                     lambda e, slab=slab, kc=kc, bk=bk: e.matmul(
                         out=ps_st[bk][:], lhsT=slab[:, kc * 128:(kc + 1) * 128],
                         rhs=h_A[:, kc * T:(kc + 1) * T],
                         start=(kc == 0), stop=(kc == KH - 1)),
                     waits=kw,
                     inc="se_pe" if kc == KH - 1 else None)
            rec[("w3mm", i, j)] = P.cnt["se_pe"]
            plan_slabB_load(g + RS_B)

            aw = [("se_pe", rec[("w3mm", i, j)])]
            if j >= 2:
                aw.append(("se_dve", rec[("gdve", i, j - 2)]))
            else:
                aw.append(("se_dve", rec[("ascr_tail", "w3", i)]))
            P.op("scalar",
                 lambda e, j=j, bk=bk: e.activation(
                     out=act_scr[j % 2][:], in_=ps_st[bk][:], func=AF.Sigmoid,
                     scale=w3s[:, j:j + 1]),
                 waits=aw, inc="se_act")
            rec[("sigw", i, j)] = P.cnt["se_act"]
            P.op("vector",
                 lambda e, j=j, bk=bk: e.scalar_tensor_tensor(
                     out=g_scr[:], in0=ps_st[bk][:], scalar=w3s[:, j:j + 1],
                     in1=act_scr[j % 2][:], op0=ALU.mult, op1=ALU.mult),
                 waits=[("se_act", rec[("sigw", i, j)])], inc="se_dve")
            rec[("gdve", i, j)] = P.cnt["se_dve"]
            P.op("vector",
                 lambda e, j=j: e.tensor_mul(
                     out=s_tile[:, j * T:(j + 1) * T], in0=s_tile[:, j * T:(j + 1) * T],
                     in1=g_scr[:]),
                 inc="se_dve")
            rec[("sg", i, j)] = P.cnt["se_dve"]

        if debug and i == 0:
            P.op("sync", lambda e: e.dma_start(out=dbg["sg0"][:], in_=s_tile[:]),
                 waits=[("se_dve", rec[("sg", 0, 7)])], inc="sm_dbg", inc_n=16)

    plan_fe_stats(0)
    plan_prologue_compute()
    plan_fe_rest(0)
    for i in range(NT):
        plan_backend_part1(i)
        if i + 1 < NT:
            plan_fe_rest(i + 1)
        plan_backend_part2(i)
    for pos in range(8):
        plan_sgT(NT - 1, pos)

    final_waits = [(f"sm_st{b}", 16 * NT) for b in range(NB)]
    if debug:
        final_waits.append(("sm_dbg", P.cnt["sm_dbg"]))
    P.op("gpsimd", lambda e: e.engine_nop(), waits=final_waits)

    with nc.Block() as block:
        @block.sync
        def _(e):
            for f in P.prog["sync"]:
                f(e)

        @block.gpsimd
        def _(e):
            for f in P.prog["gpsimd"]:
                f(e)

        @block.tensor
        def _(e):
            for f in P.prog["tensor"]:
                f(e)

        @block.vector
        def _(e):
            for f in P.prog["vector"]:
                f(e)

        @block.scalar
        def _(e):
            for f in P.prog["scalar"]:
                f(e)

    return nc, sbuf_left


def _layout_inputs(inputs):
    f32 = np.float32
    u = np.ascontiguousarray(np.asarray(inputs["L0_attn_u"], dtype=f32))

    def slabs(w, nj, nkc):
        w = np.asarray(w, dtype=f32).reshape(nj, 128, nkc, 128)
        return w.transpose(0, 3, 2, 1)

    wA = np.ascontiguousarray(np.concatenate(
        [slabs(inputs["L0_ffn_w1_w"], 32, 8), slabs(inputs["L0_ffn_w2_w"], 32, 8)],
        axis=2))
    wH1 = np.ascontiguousarray(slabs(inputs["L0_ffn_h1_w"], 32, 32))
    wH2 = np.ascontiguousarray(slabs(inputs["L0_ffn_h2_w"], 32, 32))
    wW3 = np.ascontiguousarray(slabs(inputs["L0_ffn_w3_w"], 8, 32))
    BT = np.ascontiguousarray(np.asarray(inputs["L0_attn_B"], dtype=f32).T)
    CT = np.ascontiguousarray(np.asarray(inputs["L0_attn_C"], dtype=f32).T)

    def colmaj(v, nj):
        return np.ascontiguousarray(np.asarray(v, dtype=f32).reshape(nj, 128).T)

    common = {
        "wA": wA, "wH1": wH1, "wH2": wH2, "wW3": wW3,
        "BT": BT, "CT": CT,
        "w1s": colmaj(inputs["L0_ffn_w1_s"], 32),
        "w2s": colmaj(inputs["L0_ffn_w2_s"], 32),
        "h1s": colmaj(inputs["L0_ffn_h1_s"], 32),
        "h2s": colmaj(inputs["L0_ffn_h2_s"], 32),
        "w3s": colmaj(inputs["L0_ffn_w3_s"], 8),
        "Dv": colmaj(inputs["L0_attn_D"], 8),
        "Lam": np.asarray(inputs["L0_attn_Lambda"], dtype=f32).reshape(N, 1),
        "logdt": np.asarray(inputs["L0_attn_log_dt"], dtype=f32).reshape(1, 1),
        "idbf": np.eye(128, dtype=ml_dtypes.bfloat16),
        "idf": np.eye(128, dtype=f32),
    }
    in_maps = []
    for b in range(8):
        m = dict(common)
        m["u"] = np.ascontiguousarray(u[b])
        in_maps.append(m)
    return in_maps


def kernel(**inputs):
    in_maps = _layout_inputs(inputs)
    nc, _ = build_program(debug=DEBUG)
    res = run_bass_kernel_spmd(nc, in_maps, core_ids=list(range(8)))
    out = np.stack([r["out"] for r in res.results], axis=0)
    if DEBUG:
        kernel._dbg = res.results
    return out
